# version 02
# baseline (speedup 1.0000x reference)
import numpy as np
import ml_dtypes
from contextlib import ExitStack

import concourse.bass as bass
import concourse.mybir as mybir
from concourse.bass_utils import run_bass_kernel_spmd

F32 = mybir.dt.float32
BF16 = mybir.dt.bfloat16
AF = mybir.ActivationFunctionType
ALU = mybir.AluOpType
AX = mybir.AxisListType

D = 2048
S = 4096
NQ = 2048
NH = 16
HD = 128
G = 2
CC = D // 128
KC = S // 128
QTW = 512
NQT = NQ // QTW
STRIP = 512
NSTRIP = S // STRIP
NE = 8
SCALE = float(HD) ** -0.5
EPS = 1e-5
NTT = NQ // 128

ENGINES = ("tensor", "vector", "scalar", "sync", "gpsimd")


class Sem:
    def __init__(self, ctx, nc, name):
        self.h = ctx.enter_context(nc.semaphore(name))
        self.n = 0
        self.name = name


class Sched:
    def __init__(self):
        self.q = {e: [] for e in ENGINES}

    def op(self, eng, fn, waits=(), incs=()):
        ev = None
        inc_list = []
        for sem, amt in incs:
            sem.n += amt
            inc_list.append((sem, amt))
            if ev is None:
                ev = (sem, sem.n)
        self.q[eng].append((tuple(w for w in waits if w is not None), fn, tuple(inc_list)))
        return ev

    def emit(self, blk):
        for eng in ENGINES:
            items = self.q[eng]
            if not items:
                continue

            def body(engine, items=items):
                for waits, fn, incs in items:
                    for sem, val in waits:
                        engine.wait_ge(sem.h, val)
                    inst = fn(engine)
                    for sem, amt in incs:
                        inst.then_inc(sem.h, amt)

            getattr(blk, eng)(body)


def build_program(npass=8, do_bc=True, dbg_attn=False, dbg_x2=False, dbg_proj=False,
                  dbg_att1=False):
    nc = bass.Bass("TRN2", target_bir_lowering=False)

    xT = nc.dram_tensor("xT", [D, S], F32, kind="ExternalInput").ap()
    xq = nc.dram_tensor("xq", [NQ, D], F32, kind="ExternalInput").ap()
    wqT = nc.dram_tensor("wqT", [D, D], F32, kind="ExternalInput").ap()
    wkT = nc.dram_tensor("wkT", [D, D], F32, kind="ExternalInput").ap()
    wvT = nc.dram_tensor("wvT", [D, D], F32, kind="ExternalInput").ap()
    woT = nc.dram_tensor("woT", [D, D], F32, kind="ExternalInput").ap()
    gwt = nc.dram_tensor("gwt", [128, CC * NE], F32, kind="ExternalInput").ap()
    gb = nc.dram_tensor("gb", [1, NE], F32, kind="ExternalInput").ap()
    ew = nc.dram_tensor("ew", [NE, D, D], F32, kind="ExternalInput").ap()
    eb = nc.dram_tensor("eb", [1, NE * D], F32, kind="ExternalInput").ap()
    ones_bf = nc.dram_tensor("ones_bf", [1, 128], BF16, kind="ExternalInput").ap()
    onescol_bf = nc.dram_tensor("onescol_bf", [128, 1], BF16, kind="ExternalInput").ap()
    onesmat_bf = nc.dram_tensor("onesmat_bf", [128, 128], BF16, kind="ExternalInput").ap()
    ones_f = nc.dram_tensor("ones_f", [1, 128], F32, kind="ExternalInput").ap()
    epscol = nc.dram_tensor("epscol", [128, 1], F32, kind="ExternalInput").ap()

    y = nc.dram_tensor("y", [NQ, D], F32, kind="ExternalOutput").ap()
    dbgA = nc.dram_tensor("dbgA", [128, NH * NQ], F32, kind="ExternalOutput").ap() if dbg_attn else None
    dbgX = nc.dram_tensor("dbgX", [NQ, D], F32, kind="ExternalOutput").ap() if dbg_x2 else None
    if dbg_proj:
        dbgK = nc.dram_tensor("dbgK", [128, G * S], F32, kind="ExternalOutput").ap()
        dbgQ = nc.dram_tensor("dbgQ", [128, G * NQ], F32, kind="ExternalOutput").ap()
        dbgV = nc.dram_tensor("dbgV", [128, KC * G * HD], F32, kind="ExternalOutput").ap()
        dbgN = nc.dram_tensor("dbgN", [128, 16], F32, kind="ExternalOutput").ap()

    if dbg_att1:
        dbgD = nc.dram_tensor("dbgD", [1, 512], F32, kind="ExternalOutput").ap()
        dbgR = nc.dram_tensor("dbgR", [128, 512], F32, kind="ExternalOutput").ap()
        dbgE = nc.dram_tensor("dbgE", [128, 512], F32, kind="ExternalOutput").ap()
        dbgE0 = nc.dram_tensor("dbgE0", [128, 512], F32, kind="ExternalOutput").ap()
        dbgE1 = nc.dram_tensor("dbgE1", [128, 512], F32, kind="ExternalOutput").ap()
        dbgU = nc.dram_tensor("dbgU", [128, 512], F32, kind="ExternalOutput").ap()

    x2_scr = nc.dram_tensor("x2_scr", [NQ, D], F32, kind="Internal").ap()
    x2bf_scr = nc.dram_tensor("x2bf_scr", [NQ, D], BF16, kind="Internal").ap()

    ctx = ExitStack()

    onesbf_sb = ctx.enter_context(nc.sbuf_tensor("onesbf_sb", [1, 128], BF16, side="right"))
    onescol_sb = ctx.enter_context(nc.sbuf_tensor("onescol_sb", [128, 1], BF16, side="right"))
    onesmat_sb = ctx.enter_context(nc.sbuf_tensor("onesmat_sb", [128, 128], BF16, side="right"))
    onesf_sb = ctx.enter_context(nc.sbuf_tensor("onesf_sb", [1, 128], F32, side="right"))
    eps_sb = ctx.enter_context(nc.sbuf_tensor("eps_sb", [128, 1], F32, side="right"))

    ps = [ctx.enter_context(nc.psum_tensor(f"ps{i}", [128, 512], F32)) for i in range(4)]
    ps_sc = [ctx.enter_context(nc.psum_tensor(f"ps_sc{i}", [128, 512], F32)) for i in range(2)]
    ps_at = ctx.enter_context(nc.psum_tensor("ps_at", [128, 512], F32))
    ps_dn = ctx.enter_context(nc.psum_tensor("ps_dn", [128, 512], F32))

    s_xt = Sem(ctx, nc, "s_xt")
    s_w = Sem(ctx, nc, "s_w")
    s_pe = Sem(ctx, nc, "s_pe")
    s_sc = Sem(ctx, nc, "s_sc")
    s_ve = Sem(ctx, nc, "s_ve")
    s_dq = Sem(ctx, nc, "s_dq")
    s_sy = Sem(ctx, nc, "s_sy")
    s_out = Sem(ctx, nc, "s_out")

    sch = Sched()
    blk = ctx.enter_context(nc.Block())

    bank_free = {}

    def wfree(bid):
        return bank_free.get(bid)

    with ctx:
        st_ab = ExitStack()
        attnT = st_ab.enter_context(nc.sbuf_tensor("attnT", [128, NH * NQ], BF16))
        st_a = ExitStack()

        def sba(name, shape, dt):
            return st_a.enter_context(nc.sbuf_tensor(name, shape, dt))

        wq_b = [sba(f"wqb{i}", [128, CC * G * HD], BF16) for i in range(2)]
        wk_b = [sba(f"wkb{i}", [128, CC * G * HD], BF16) for i in range(2)]
        wv_b = [sba(f"wvb{i}", [128, CC * G * HD], BF16) for i in range(2)]
        xt_b = [sba(f"xtb{i}", [128, CC * STRIP], BF16) for i in range(2)]
        KT = sba("KT", [128, G * S], BF16)
        QTm = sba("QTm", [128, G * NQ], BF16)
        Vm = sba("Vm", [128, KC * G * HD], BF16)
        nsc = sba("nsc", [128, 2 * 512], BF16)
        runmx = sba("runmx", [1, 8], F32)
        sc11 = sba("sc11", [1, 8], F32)
        negc = sba("negc", [1, G], F32)
        negc_col = sba("negc_col", [128, G], F32)
        expS = [sba(f"expS{i}", [128, 512], BF16) for i in range(2)]
        dnln = sba("dnln", [128, 512], F32)
        recip_rep = sba("recip_rep", [128, 512], F32)

        for t_sb, t_dr in ((onesbf_sb, ones_bf), (onescol_sb, onescol_bf),
                           (onesmat_sb, onesmat_bf), (onesf_sb, ones_f), (eps_sb, epscol)):
            sch.op("gpsimd", lambda g, a=t_sb, b=t_dr: g.dma_start(a[:, :], b),
                   incs=[(s_w, 16)])

        def w_slice_dma(dst, srcT, p, waits=()):
            src = srcT.rearrange("(c p) d -> p c d", p=128)[:, :, p * G * HD:(p + 1) * G * HD]
            d3 = dst[:, :].rearrange("p (c n) -> p c n", c=CC)
            ev = None
            for half in range(2):
                cs = slice(half * 8, (half + 1) * 8)
                ev = sch.op("gpsimd",
                            lambda g, d=d3[:, cs, :], s2=src[:, cs, :]: g.dma_start(d, s2),
                            waits=(waits if half == 0 else ()), incs=[(s_w, 16)])
            return ev

        w_slice_dma(wq_b[0], wqT, 0)
        w_slice_dma(wk_b[0], wkT, 0)
        w_ready = [w_slice_dma(wv_b[0], wvT, 0)]

        xT_r = xT.rearrange("(c p) t -> p c t", p=128)

        strip_ev = {}
        strip_done = {}
        pass_proj_last = {}
        kq_evac_last = [None]
        attn_ev = []
        exp_free = {}
        expS_free = {}
        dump_gate = {}
        gchunk = [0]
        d_ev_last = [None]

        for p in range(npass):
            wq, wk, wv = wq_b[p % 2], wk_b[p % 2], wv_b[p % 2]
            grp_ctr = [0]
            norm_ctr = [0]
            nsc_free = {}

            mx_init_ev = sch.op("vector", lambda v: v.memset(runmx[:, 0:4], -1e30),
                                waits=[attn_ev[-1]] if attn_ev else (), incs=[(s_ve, 1)])
            norm_last = {0: mx_init_ev, 1: mx_init_ev, 2: mx_init_ev, 3: mx_init_ev}

            def proj_group(p, s, lhs_fn, rhs_fn, n_out, dst, doff, evac_eng):
                g_i = grp_ctr[0] % 4
                grp_ctr[0] += 1
                waits = [strip_ev[(p, s)], wfree(g_i), w_ready[p]]

                def fn(t, bank=ps[g_i], lhs_fn=lhs_fn, rhs_fn=rhs_fn, n_out=n_out):
                    inst = None
                    for c in range(CC):
                        inst = t.matmul(bank[:, 0:n_out], lhs_fn(c), rhs_fn(c),
                                        start=(c == 0), stop=(c == CC - 1))
                    return inst

                mm_ev = sch.op("tensor", fn, waits=waits, incs=[(s_pe, 1)])
                strip_done[(p, s)] = s_pe.n
                sem = s_ve if evac_eng == "vector" else s_sc

                def ev_fn(e, dst=dst, doff=doff, n_out=n_out, bank=ps[g_i]):
                    if evac_eng == "vector":
                        return e.tensor_copy(dst[:, doff:doff + n_out], bank[:, 0:n_out])
                    return e.copy(dst[:, doff:doff + n_out], bank[:, 0:n_out])

                evac_ev = sch.op(evac_eng, ev_fn, waits=[mm_ev], incs=[(sem, 1)])
                bank_free[g_i] = evac_ev
                return evac_ev

            def sq_and_norm(p, src, soff, col, prior_ev):
                ping = norm_ctr[0] % 2
                norm_ctr[0] += 1
                waits = [prior_ev, nsc_free.get(ping)]
                sq_ev = sch.op("scalar",
                               lambda sc, src=src, soff=soff, ping=ping: sc.activation(
                                   nsc[:, ping * 512:(ping + 1) * 512],
                                   src[:, soff:soff + 512], AF.Square),
                               waits=waits, incs=[(s_sc, 1)])
                mm_ev = sch.op("tensor",
                               lambda t, ping=ping: t.matmul(
                                   ps_dn[0:1, :], onescol_sb[:, :],
                                   nsc[:, ping * 512:(ping + 1) * 512],
                                   start=True, stop=True),
                               waits=[sq_ev, wfree("dn")], incs=[(s_pe, 1)])
                nsc_free[ping] = mm_ev
                r_ev = sch.op("vector",
                              lambda v: v.reduce_max(sc11[:, 7:8], ps_dn[0:1, :], axis=AX.X),
                              waits=[mm_ev, norm_last[col]], incs=[(s_ve, 1)])
                bank_free["dn"] = r_ev
                m_ev = sch.op("vector",
                              lambda v, col=col: v.tensor_tensor(
                                  runmx[:, col:col + 1], runmx[:, col:col + 1],
                                  sc11[:, 7:8], ALU.max),
                              waits=[r_ev], incs=[(s_ve, 1)])
                norm_last[col] = m_ev

            for s in range(NSTRIP):
                waits = []
                if (p, s) != (0, 0) and (p, s) != (0, 1):
                    pp, ss = (p, s - 2) if s >= 2 else (p - 1, NSTRIP + s - 2)
                    waits = [(s_pe, strip_done[(pp, ss)])]
                dst = xt_b[s % 2][:, :].rearrange("p (c n) -> p c n", c=CC)
                src = xT_r[:, :, s * STRIP:(s + 1) * STRIP]
                evx = None
                for half in range(2):
                    cs = slice(half * 8, (half + 1) * 8)
                    evx = sch.op("gpsimd",
                                 lambda g, d=dst[:, cs, :], sr=src[:, cs, :]: g.dma_start(d, sr),
                                 waits=(waits if half == 0 else ()), incs=[(s_xt, 16)])
                strip_ev[(p, s)] = evx
                xb = xt_b[s % 2]

                for dc in range(G):
                    e1 = proj_group(
                        p, s,
                        lambda c, dc=dc, wk=wk: wk[:, c * G * HD + dc * HD: c * G * HD + (dc + 1) * HD],
                        lambda c, xb=xb: xb[:, c * STRIP:(c + 1) * STRIP],
                        STRIP, KT, dc * S + s * STRIP, "vector")
                    kq_evac_last[0] = e1
                    sq_and_norm(p, KT, dc * S + s * STRIP, 2 + dc, e1)
                if s < NQ // STRIP:
                    for dc in range(G):
                        e1 = proj_group(
                            p, s,
                            lambda c, dc=dc, wq=wq: wq[:, c * G * HD + dc * HD: c * G * HD + (dc + 1) * HD],
                            lambda c, xb=xb: xb[:, c * STRIP:(c + 1) * STRIP],
                            STRIP, QTm, dc * NQ + s * STRIP, "vector")
                        kq_evac_last[0] = e1
                        sq_and_norm(p, QTm, dc * NQ + s * STRIP, dc, e1)
                for tt in range(STRIP // 128):
                    ktile = s * (STRIP // 128) + tt
                    proj_group(
                        p, s,
                        lambda c, xb=xb, tt=tt: xb[:, c * STRIP + tt * 128: c * STRIP + (tt + 1) * 128],
                        lambda c, wv=wv: wv[:, c * G * HD:(c + 1) * G * HD],
                        G * HD, Vm, ktile * G * HD, "scalar")

            pass_proj_last[p] = s_pe.n

            if p + 1 < npass:
                nxt = (p + 1) % 2
                w0 = [(s_pe, pass_proj_last[p - 1])] if p >= 1 else ()
                w_slice_dma(wq_b[nxt], wqT, p + 1, waits=w0)
                w_slice_dma(wk_b[nxt], wkT, p + 1)
                w_ready.append(w_slice_dma(wv_b[nxt], wvT, p + 1))

            neg_evs = []
            for h in range(G):
                e1 = sch.op("scalar",
                            lambda sc, h=h: sc.activation(sc11[:, 0:1], runmx[:, h:h + 1], AF.Ln),
                            waits=[norm_last[h], norm_last[2 + h]], incs=[(s_sc, 1)])
                e2 = sch.op("scalar",
                            lambda sc, h=h: sc.activation(sc11[:, 1:2], runmx[:, 2 + h:3 + h], AF.Ln),
                            waits=[e1], incs=[(s_sc, 1)])
                e3 = sch.op("vector",
                            lambda v: v.tensor_tensor(sc11[:, 2:3], sc11[:, 0:1], sc11[:, 1:2], ALU.add),
                            waits=[e2], incs=[(s_ve, 1)])
                e4 = sch.op("scalar",
                            lambda sc: sc.activation(sc11[:, 3:4], sc11[:, 2:3], AF.Exp, scale=0.5),
                            waits=[e3], incs=[(s_sc, 1)])
                e5 = sch.op("vector",
                            lambda v, h=h: v.tensor_scalar(
                                negc[:, h:h + 1], sc11[:, 3:4], -1.01 * SCALE, None, ALU.mult),
                            waits=[e4], incs=[(s_ve, 1)])
                neg_evs.append(e5)
            rep_mm = sch.op("tensor",
                            lambda t: t.matmul(ps_at[:, 0:G], onesf_sb[:, :], negc[:, 0:G],
                                               start=True, stop=True),
                            waits=neg_evs + [wfree("at")], incs=[(s_pe, 1)])
            rep_cp = sch.op("scalar",
                            lambda sc: sc.copy(negc_col[:, :], ps_at[:, 0:G]),
                            waits=[rep_mm], incs=[(s_sc, 1)])
            bank_free["at"] = rep_cp

            for h in range(1 if dbg_att1 else G):
                hd = p * G + h
                for qt in range(1 if dbg_att1 else NQT):
                    qoff = qt * QTW
                    pend = []

                    def emit_attn_chunk(kc, ping, e_ev):
                        aw = [e_ev]
                        if kc == 0:
                            aw.append(wfree("at"))

                        def mm_a(t, kc=kc, h=h, ping=ping):
                            return t.matmul(ps_at[:, :],
                                            Vm[:, kc * G * HD + h * HD: kc * G * HD + (h + 1) * HD],
                                            expS[ping][:, :],
                                            start=(kc == 0), stop=(kc == KC - 1))

                        a_ev = sch.op("tensor", mm_a, waits=aw, incs=[(s_pe, 1)])
                        dw = [wfree("dn")] if kc == 0 else []

                        def mm_d(t, kc=kc, ping=ping):
                            return t.matmul(ps_dn[:, :], onesmat_sb[:, :], expS[ping][:, :],
                                            start=(kc == 0), stop=(kc == KC - 1))

                        d_ev = sch.op("tensor", mm_d, waits=dw, incs=[(s_pe, 1)])
                        expS_free[ping] = d_ev
                        return a_ev, d_ev

                    a_ev = d_ev = None
                    for kc in range(KC):
                        ping = gchunk[0] % 2
                        gchunk[0] += 1
                        sw = [exp_free.get(ping)]
                        if kc == 0 and qt == 0 and h == 0:
                            sw.append(kq_evac_last[0])

                        def mm_s(t, h=h, kc=kc, qoff=qoff, ping=ping):
                            return t.matmul(ps_sc[ping][:, :],
                                            KT[:, h * S + kc * 128:h * S + (kc + 1) * 128],
                                            QTm[:, h * NQ + qoff:h * NQ + qoff + QTW],
                                            start=True, stop=True)

                        s_ev = sch.op("tensor", mm_s, waits=sw, incs=[(s_pe, 1)])

                        ew_ = [s_ev, expS_free.get(ping), dump_gate.pop(ping, None)]
                        if kc == 0 and qt == 0 and h == 0:
                            ew_.append(rep_cp)
                        e_ev = sch.op("scalar",
                                      lambda sc, ping=ping, h=h: sc.activation(
                                          expS[ping][:, :], ps_sc[ping][:, :], AF.Exp,
                                          bias=negc_col[:, h:h + 1], scale=SCALE),
                                      waits=ew_, incs=[(s_sc, 1)])
                        exp_free[ping] = e_ev
                        if dbg_att1 and p == 0 and h == 0 and qt == 0 and kc in (0, 16):
                            dst = dbgE0 if kc == 0 else dbgE1
                            dev_ = sch.op("gpsimd",
                                          lambda g, dst=dst, ping=ping: g.dma_start(
                                              dst, expS[ping][:, :]),
                                          waits=[e_ev], incs=[(s_out, 16)])
                            dump_gate[ping] = dev_
                        pend.append((kc, ping, e_ev))
                        if len(pend) >= 2:
                            kc0, ping0, e0 = pend.pop(0)
                            a_ev, d_ev = emit_attn_chunk(kc0, ping0, e0)
                    while pend:
                        kc0, ping0, e0 = pend.pop(0)
                        a_ev, d_ev = emit_attn_chunk(kc0, ping0, e0)

                    r1 = sch.op("scalar",
                                lambda sc: sc.copy(dnln[:, :], ps_dn[:, :]),
                                waits=[d_ev], incs=[(s_sc, 1)])
                    bank_free["dn"] = r1
                    r2w = [r1, attn_ev[-1] if attn_ev else None]
                    r2 = sch.op("vector",
                                lambda v: v.reciprocal(recip_rep[:, :], dnln[:, :]),
                                waits=r2w, incs=[(s_ve, 1)])
                    t_ev = sch.op("vector",
                                  lambda v, hd=hd, qoff=qoff: v.tensor_tensor(
                                      attnT[:, hd * NQ + qoff: hd * NQ + qoff + QTW],
                                      ps_at[:, :], recip_rep[:, :], ALU.mult),
                                  waits=[a_ev, r2], incs=[(s_ve, 1)])
                    attn_ev.append(t_ev)
                    bank_free["at"] = t_ev
                    if dbg_att1 and p == 0 and h == 0 and qt == 0:
                        sch.op("gpsimd", lambda g: g.dma_start(dbgR, recip_rep[:, :]),
                               waits=[t_ev], incs=[(s_out, 16)])
                        sch.op("gpsimd",
                               lambda g: g.dma_start(dbgE, expS[(gchunk[0] - 1) % 2][:, :]),
                               waits=[t_ev], incs=[(s_out, 16)])

        phA = [(s_pe, s_pe.n), (s_ve, s_ve.n), (s_sc, s_sc.n)]

        if dbg_proj:
            wv_ = [(s_pe, s_pe.n), (s_ve, s_ve.n), (s_sc, s_sc.n)]
            sch.op("gpsimd", lambda g: g.dma_start(dbgK, KT[:, :]), waits=wv_, incs=[(s_out, 16)])
            sch.op("gpsimd", lambda g: g.dma_start(dbgQ, QTm[:, :]), incs=[(s_out, 16)])
            sch.op("gpsimd", lambda g: g.dma_start(dbgV, Vm[:, :]), incs=[(s_out, 16)])
            sch.op("gpsimd", lambda g: g.dma_start(dbgN[:, 0:G], negc_col[:, :]), incs=[(s_out, 16)])
            sch.op("gpsimd", lambda g: g.dma_start(dbgN[0:1, 8:16], runmx[:, :]), incs=[(s_out, 16)])

        if dbg_attn:
            dmp = sch.op("gpsimd", lambda g: g.dma_start(dbgA, attnT[:, :]),
                         waits=[attn_ev[-1]], incs=[(s_out, 16)])
            sch.op("sync", lambda s2: s2.nop(), waits=[(s_out, s_out.n)])

        if not do_bc:
            sch.emit(blk)
            st_a.close()
            st_ab.close()
            return nc

        st_a.close()
        X2T = ctx.enter_context(nc.sbuf_tensor("X2T", [128, CC * 1024], BF16, side="right"))
        x2bf = [ctx.enter_context(nc.sbuf_tensor(f"x2bf{i}", [128, D], BF16, side="right"))
                for i in range(2)]
        st_b = ExitStack()
        woT_b = st_b.enter_context(nc.sbuf_tensor("woT_b", [128, CC * D], BF16))
        xres = [st_b.enter_context(nc.sbuf_tensor(f"xres{i}", [128, D], F32)) for i in range(2)]
        cen = [st_b.enter_context(nc.sbuf_tensor(f"cen{i}", [128, D], F32)) for i in range(2)]
        nmean = st_b.enter_context(nc.sbuf_tensor("nmean", [128, 1], F32))
        sumsq = st_b.enter_context(nc.sbuf_tensor("sumsq", [128, 1], F32))
        rstd = st_b.enter_context(nc.sbuf_tensor("rstd", [128, 1], F32))
        sqd = st_b.enter_context(nc.sbuf_tensor("sqd", [128, D], BF16))

        woT_ev = []
        for hdc in range(CC):
            wv_ = phA if hdc == 0 else ()
            woT_ev.append(sch.op("gpsimd",
                                 lambda g, hdc=hdc: g.dma_start(
                                     woT_b[:, hdc * D:(hdc + 1) * D],
                                     woT[hdc * 128:(hdc + 1) * 128, :]),
                                 waits=wv_, incs=[(s_w, 16)]))

        resid_ev = {}
        cen_free = {}
        x2w_ev = {}
        x2bf_free = {}
        tposed_ev = {}
        ln_chain = [None]

        for tt in range(NTT):
            pi = tt % 2
            xr_ev = sch.op("scalar",
                           lambda sc, tt=tt, pi=pi: sc.dma_start(
                               xres[pi][:, :], xq[tt * 128:(tt + 1) * 128, :]),
                           waits=[resid_ev.get(tt - 2)], incs=[(s_dq, 16)])
            g_evs = []
            for dmt in range(4):
                waits = [wfree(dmt)]
                if tt == 0:
                    waits += [woT_ev[-1]] + phA

                def fn(t, dmt=dmt, tt=tt):
                    inst = None
                    for hdc in range(CC):
                        inst = t.matmul(ps[dmt][:, :],
                                        attnT[:, hdc * NQ + tt * 128:hdc * NQ + (tt + 1) * 128],
                                        woT_b[:, hdc * D + dmt * 512:hdc * D + (dmt + 1) * 512],
                                        start=(hdc == 0), stop=(hdc == CC - 1))
                    return inst

                g_evs.append(sch.op("tensor", fn, waits=waits, incs=[(s_pe, 1)]))
            r_evs = []
            for dmt in range(4):
                r_evs.append(sch.op("vector",
                                    lambda v, dmt=dmt, pi=pi: v.tensor_tensor(
                                        cen[pi][:, dmt * 512:(dmt + 1) * 512],
                                        ps[dmt][:, :],
                                        xres[pi][:, dmt * 512:(dmt + 1) * 512], ALU.add),
                                    waits=[g_evs[dmt], xr_ev, cen_free.get(tt - 2)],
                                    incs=[(s_ve, 1)]))
                bank_free[dmt] = r_evs[-1]
            resid_ev[tt] = r_evs[-1]

            l1 = sch.op("vector",
                        lambda v, pi=pi: v.reduce_sum(nmean[:, :], cen[pi][:, :], axis=AX.X, negate=True),
                        waits=r_evs + [ln_chain[0]], incs=[(s_ve, 1)])
            l2 = sch.op("vector",
                        lambda v: v.tensor_scalar(nmean[:, :], nmean[:, :], 1.0 / D, None, ALU.mult),
                        waits=[l1], incs=[(s_ve, 1)])
            l3 = sch.op("vector",
                        lambda v, pi=pi: v.tensor_scalar(cen[pi][:, :], cen[pi][:, :],
                                                         nmean[:, :], None, ALU.add),
                        waits=[l2], incs=[(s_ve, 1)])
            l4 = sch.op("scalar",
                        lambda sc, pi=pi: sc.activation(sqd[:, :], cen[pi][:, :], AF.Square,
                                                        accum_out=sumsq[:, :]),
                        waits=[l3], incs=[(s_sc, 1)])
            l5 = sch.op("scalar",
                        lambda sc: sc.activation(sumsq[:, :], sumsq[:, :], AF.Ln,
                                                 bias=eps_sb[:, 0:1], scale=1.0 / D),
                        waits=[l4], incs=[(s_sc, 1)])
            l6 = sch.op("scalar",
                        lambda sc: sc.activation(rstd[:, :], sumsq[:, :], AF.Exp, scale=-0.5),
                        waits=[l5], incs=[(s_sc, 1)])
            l7 = sch.op("vector",
                        lambda v, pi=pi: v.tensor_scalar(cen[pi][:, :], cen[pi][:, :],
                                                         rstd[:, :], None, ALU.mult),
                        waits=[l6], incs=[(s_ve, 1)])
            l8 = sch.op("vector",
                        lambda v, pi=pi: v.tensor_copy(x2bf[pi][:, :], cen[pi][:, :]),
                        waits=[l7, x2bf_free.get(tt - 2)], incs=[(s_ve, 1)])
            ln_chain[0] = l8

            w1 = sch.op("scalar",
                        lambda sc, tt=tt, pi=pi: sc.dma_start(
                            x2_scr[tt * 128:(tt + 1) * 128, :], cen[pi][:, :]),
                        waits=[l7], incs=[(s_dq, 16)])
            x2w_ev[tt] = w1
            cen_free[tt] = w1
            if tt >= 8:
                w2 = sch.op("scalar",
                            lambda sc, tt=tt, pi=pi: sc.dma_start(
                                x2bf_scr[tt * 128:(tt + 1) * 128, :], x2bf[pi][:, :]),
                            waits=[l8], incs=[(s_dq, 16)])
                x2bf_free[tt] = w2
            else:
                tp = None
                for c in range(CC):
                    tp = sch.op("sync",
                                lambda s2, c=c, tt=tt, pi=pi: s2.dma_start(
                                    X2T[:, c * 1024 + tt * 128:c * 1024 + (tt + 1) * 128],
                                    x2bf[pi][:, c * 128:(c + 1) * 128], transpose=True),
                                waits=[l8] if c == 0 else (), incs=[(s_sy, 16)])
                tposed_ev[tt] = tp
                x2bf_free[tt] = tp

            if dbg_x2:
                sch.op("gpsimd",
                       lambda g, tt=tt, pi=pi: g.dma_start(
                           dbgX[tt * 128:(tt + 1) * 128, :], cen[pi][:, :]),
                       waits=[l7], incs=[(s_out, 16)])

        phB = [(s_pe, s_pe.n), (s_ve, s_ve.n), (s_sc, s_sc.n), (s_dq, s_dq.n), (s_sy, s_sy.n)]

        st_b.close()
        st_ab.close()
        st_c = ExitStack()

        def sbc(name, shape, dt):
            return st_c.enter_context(nc.sbuf_tensor(name, shape, dt))

        acc = sbc("acc", [128, 8 * D], F32)
        We_b = [sbc(f"Web{i}", [128, CC * 512], BF16) for i in range(2)]
        tmp = [sbc(f"tmp{i}", [128, 512], F32) for i in range(2)]
        yb = [sbc(f"yb{i}", [128, D], F32) for i in range(2)]
        gwb = sbc("gwb", [128, CC * NE], BF16)
        gbb = sbc("gbb", [1, NE], BF16)
        ebb = [sbc(f"ebb{i}", [1, D], BF16) for i in range(2)]
        zg = sbc("zg", [128, NE], F32)
        zz = sbc("zz", [128, NE], F32)
        m1 = sbc("m1", [128, 1], F32)
        m2 = sbc("m2", [128, 1], F32)
        mask = sbc("mask", [128, 8 * NE], F32)
        nmean2 = sbc("nmean2", [128, 1], F32)
        sumsq2 = sbc("sumsq2", [128, 1], F32)
        rstd2 = sbc("rstd2", [128, 1], F32)
        sqd2 = sbc("sqd2", [128, D], BF16)

        gw_ev = sch.op("gpsimd", lambda g: g.dma_start(gwb[:, :], gwt), waits=phB, incs=[(s_w, 16)])
        gb_ev = sch.op("gpsimd", lambda g: g.dma_start(gbb[:, :], gb), incs=[(s_w, 16)])

        grp_c = [0]
        acc_w = {}
        tmp_free = {}
        We_free = {}
        ebb_free = {}
        yb_free = {}
        acc_init_ev = {}
        ln2_chain = [None]
        halfC_pe_done = [None]
        tpose_c = {}
        x2bf_ld = {}
        last_y = []

        for H in range(2):
            if H == 1:
                for ti in range(8):
                    tt = 8 + ti
                    pi = tt % 2
                    ld = sch.op("scalar",
                                lambda sc, tt=tt, pi=pi: sc.dma_start(
                                    x2bf[pi][:, :], x2bf_scr[tt * 128:(tt + 1) * 128, :]),
                                waits=[x2bf_free.get(tt - 2), x2w_ev.get(tt),
                                       (s_dq, phB[3][1])], incs=[(s_dq, 16)])
                    tp = None
                    for c in range(CC):
                        wv_ = [ld]
                        if ti == 0 and c == 0:
                            wv_.append((s_pe, halfC_pe_done[0]))
                        tp = sch.op("sync",
                                    lambda s2, c=c, ti=ti, pi=pi: s2.dma_start(
                                        X2T[:, c * 1024 + ti * 128:c * 1024 + (ti + 1) * 128],
                                        x2bf[pi][:, c * 128:(c + 1) * 128], transpose=True),
                                    waits=wv_, incs=[(s_sy, 16)])
                    tpose_c[tt] = tp
                    x2bf_free[tt] = tp

            for ti in range(8):
                tt = H * 8 + ti
                wv_ = [x2w_ev[tt], acc_w.get(("ln2", ti))]
                if H == 0 and ti == 0:
                    wv_ += phB
                acc_init_ev[ti] = sch.op("scalar",
                                         lambda sc, tt=tt, ti=ti: sc.dma_start(
                                             acc[:, ti * D:(ti + 1) * D],
                                             x2_scr[tt * 128:(tt + 1) * 128, :]),
                                         waits=wv_, incs=[(s_dq, 16)])

            mask_ev = {}
            for ti in range(8):
                tt = H * 8 + ti
                g_i = grp_c[0] % 4
                grp_c[0] += 1
                wv_ = [gw_ev, gb_ev, wfree(g_i),
                       tposed_ev.get(tt) if H == 0 else tpose_c.get(tt)]

                def fng(t, ti=ti, g_i=g_i):
                    t.matmul(ps[g_i][:, 0:NE], onesbf_sb[:, :], gbb[:, :], start=True, stop=False)
                    inst = None
                    for c in range(CC):
                        inst = t.matmul(ps[g_i][:, 0:NE],
                                        X2T[:, c * 1024 + ti * 128:c * 1024 + (ti + 1) * 128],
                                        gwb[:, c * NE:(c + 1) * NE],
                                        start=False, stop=(c == CC - 1))
                    return inst

                gm = sch.op("tensor", fng, waits=wv_, incs=[(s_pe, 1)])
                zev = sch.op("scalar", lambda sc, g_i=g_i: sc.copy(zg[:, :], ps[g_i][:, 0:NE]),
                             waits=[gm, mask_ev.get(ti - 1)], incs=[(s_sc, 1)])
                bank_free[g_i] = zev
                c1 = sch.op("vector", lambda v: v.reduce_max(m1[:, :], zg[:, :], axis=AX.X),
                            waits=[zev], incs=[(s_ve, 1)])
                c2 = sch.op("vector",
                            lambda v: v.tensor_scalar(zz[:, :], zg[:, :], m1[:, :], None, ALU.is_equal),
                            waits=[c1], incs=[(s_ve, 1)])
                c3 = sch.op("vector",
                            lambda v: v.scalar_tensor_tensor(zz[:, :], zz[:, :], -1e30, zg[:, :],
                                                             ALU.mult, ALU.add),
                            waits=[c2], incs=[(s_ve, 1)])
                c4 = sch.op("vector", lambda v: v.reduce_max(m2[:, :], zz[:, :], axis=AX.X),
                            waits=[c3], incs=[(s_ve, 1)])
                c5 = sch.op("vector",
                            lambda v, ti=ti: v.tensor_scalar(mask[:, ti * NE:(ti + 1) * NE],
                                                             zg[:, :], m2[:, :], 0.5,
                                                             ALU.is_ge, ALU.mult),
                            waits=[c4], incs=[(s_ve, 1)])
                mask_ev[ti] = c5

            for e in range(NE):
                wpi = e % 2
                eb_ev = sch.op("gpsimd",
                               lambda g, e=e, wpi=wpi: g.dma_start(
                                   ebb[wpi][:, :], eb[0:1, e * D:(e + 1) * D]),
                               waits=[ebb_free.get(wpi)], incs=[(s_w, 16)])
                for oq in range(4):
                    bpi = (e * 4 + oq) % 2
                    src = ew[e].rearrange("(c p) o -> p c o", p=128)[:, :, oq * 512:(oq + 1) * 512]
                    d3 = We_b[bpi][:, :].rearrange("p (c n) -> p c n", c=CC)
                    we_ev = None
                    for half in range(2):
                        cs = slice(half * 8, (half + 1) * 8)
                        we_ev = sch.op("gpsimd",
                                       lambda g, d=d3[:, cs, :], s2=src[:, cs, :]: g.dma_start(d, s2),
                                       waits=[We_free.get(bpi)] if half == 0 else (),
                                       incs=[(s_w, 16)])
                    for ti in range(8):
                        g_i = grp_c[0] % 4
                        grp_c[0] += 1
                        wv_ = [we_ev, eb_ev, wfree(g_i)]

                        def fne(t, ti=ti, g_i=g_i, e=e, oq=oq, wpi=wpi, bpi=bpi):
                            t.matmul(ps[g_i][:, :], onesbf_sb[:, :],
                                     ebb[wpi][:, oq * 512:(oq + 1) * 512], start=True, stop=False)
                            inst = None
                            for c in range(CC):
                                inst = t.matmul(ps[g_i][:, :],
                                                X2T[:, c * 1024 + ti * 128:c * 1024 + (ti + 1) * 128],
                                                We_b[bpi][:, c * 512:(c + 1) * 512],
                                                start=False, stop=(c == CC - 1))
                            return inst

                        hm = sch.op("tensor", fne, waits=wv_, incs=[(s_pe, 1)])
                        if ti == 7:
                            We_free[bpi] = hm
                            if oq == 3:
                                ebb_free[wpi] = hm
                        tpi = grp_c[0] % 2
                        rl = sch.op("scalar",
                                    lambda sc, g_i=g_i, ti=ti, e=e, tpi=tpi: sc.activation(
                                        tmp[tpi][:, :], ps[g_i][:, :], AF.Relu,
                                        scale=mask[:, ti * NE + e:ti * NE + e + 1]),
                                    waits=[hm, mask_ev[ti], tmp_free.get(tpi)],
                                    incs=[(s_sc, 1)])
                        bank_free[g_i] = rl
                        ac = sch.op("vector",
                                    lambda v, ti=ti, oq=oq, tpi=tpi: v.tensor_tensor(
                                        acc[:, ti * D + oq * 512:ti * D + (oq + 1) * 512],
                                        acc[:, ti * D + oq * 512:ti * D + (oq + 1) * 512],
                                        tmp[tpi][:, :], ALU.add),
                                    waits=[rl, acc_w.get((ti, oq), acc_init_ev[ti])],
                                    incs=[(s_ve, 1)])
                        acc_w[(ti, oq)] = ac
                        tmp_free[tpi] = ac
            halfC_pe_done[0] = s_pe.n

            for ti in range(8):
                tt = H * 8 + ti
                pi = ti % 2
                aw = [acc_w[(ti, oq)] for oq in range(4)]
                l1 = sch.op("vector",
                            lambda v, ti=ti: v.reduce_sum(nmean2[:, :], acc[:, ti * D:(ti + 1) * D],
                                                          axis=AX.X, negate=True),
                            waits=aw + [ln2_chain[0]], incs=[(s_ve, 1)])
                l2 = sch.op("vector",
                            lambda v: v.tensor_scalar(nmean2[:, :], nmean2[:, :], 1.0 / D, None, ALU.mult),
                            waits=[l1], incs=[(s_ve, 1)])
                l3 = sch.op("vector",
                            lambda v, ti=ti: v.tensor_scalar(acc[:, ti * D:(ti + 1) * D],
                                                             acc[:, ti * D:(ti + 1) * D],
                                                             nmean2[:, :], None, ALU.add),
                            waits=[l2], incs=[(s_ve, 1)])
                l4 = sch.op("scalar",
                            lambda sc, ti=ti: sc.activation(sqd2[:, :], acc[:, ti * D:(ti + 1) * D],
                                                            AF.Square, accum_out=sumsq2[:, :]),
                            waits=[l3], incs=[(s_sc, 1)])
                l5 = sch.op("scalar",
                            lambda sc: sc.activation(sumsq2[:, :], sumsq2[:, :], AF.Ln,
                                                     bias=eps_sb[:, 0:1], scale=1.0 / D),
                            waits=[l4], incs=[(s_sc, 1)])
                l6 = sch.op("scalar",
                            lambda sc: sc.activation(rstd2[:, :], sumsq2[:, :], AF.Exp, scale=-0.5),
                            waits=[l5], incs=[(s_sc, 1)])
                l7 = sch.op("vector",
                            lambda v, ti=ti, pi=pi: v.tensor_scalar(
                                yb[pi][:, :], acc[:, ti * D:(ti + 1) * D],
                                rstd2[:, :], None, ALU.mult),
                            waits=[l6, yb_free.get(pi)], incs=[(s_ve, 1)])
                ln2_chain[0] = l7
                acc_w[("ln2", ti)] = l7
                yw = sch.op("scalar",
                            lambda sc, tt=tt, pi=pi: sc.dma_start(
                                y[tt * 128:(tt + 1) * 128, :], yb[pi][:, :]),
                            waits=[l7], incs=[(s_out, 16)])
                yb_free[pi] = yw
                last_y.append(yw)

        sch.op("sync", lambda s2: s2.nop(), waits=[(s_out, s_out.n)])

        sch.emit(blk)
        st_c.close()
    return nc


_NC_CACHE = {}


def _get_nc():
    if "nc" not in _NC_CACHE:
        _NC_CACHE["nc"] = build_program()
    return _NC_CACHE["nc"]


def kernel(L0_attn_hidden_states, L0_attn_Wq, L0_attn_Wk, L0_attn_Wv, L0_attn_Wo,
           L0_ffn_gate_w, L0_ffn_gate_b, L0_ffn_expert_w, L0_ffn_expert_b):
    x = np.ascontiguousarray(np.asarray(L0_attn_hidden_states, dtype=np.float32))
    Wq = np.asarray(L0_attn_Wq, dtype=np.float32)
    Wk = np.asarray(L0_attn_Wk, dtype=np.float32)
    Wv = np.asarray(L0_attn_Wv, dtype=np.float32)
    Wo = np.asarray(L0_attn_Wo, dtype=np.float32)
    gate_w = np.asarray(L0_ffn_gate_w, dtype=np.float32)
    gate_b = np.asarray(L0_ffn_gate_b, dtype=np.float32)
    expert_w = np.ascontiguousarray(np.asarray(L0_ffn_expert_w, dtype=np.float32))
    expert_b = np.asarray(L0_ffn_expert_b, dtype=np.float32)

    wqT = np.ascontiguousarray(Wq.T)
    wkT = np.ascontiguousarray(Wk.T)
    wvT = np.ascontiguousarray(Wv.T)
    woT = np.ascontiguousarray(Wo.T)
    gwt = np.ascontiguousarray(gate_w.reshape(CC, 128, NE).transpose(1, 0, 2).reshape(128, CC * NE))
    gb1 = np.ascontiguousarray(gate_b.reshape(1, NE))
    eb1 = np.ascontiguousarray(expert_b.reshape(1, NE * D))
    ones_bf = np.ones((1, 128), dtype=ml_dtypes.bfloat16)
    onescol_bf = np.ones((128, 1), dtype=ml_dtypes.bfloat16)
    onesmat_bf1 = np.ones((128, 128), dtype=ml_dtypes.bfloat16)
    ones_f = np.ones((1, 128), dtype=np.float32)
    epscol = np.full((128, 1), EPS, dtype=np.float32)

    nc = _get_nc()
    in_maps = []
    for core in range(8):
        b, hf = core // 2, core % 2
        own = x[b, hf * NQ:(hf + 1) * NQ]
        oth = x[b, (1 - hf) * NQ:(2 - hf) * NQ]
        xroll = np.concatenate([own, oth], axis=0)
        in_maps.append({
            "xT": np.ascontiguousarray(xroll.T),
            "xq": np.ascontiguousarray(own),
            "wqT": wqT, "wkT": wkT, "wvT": wvT, "woT": woT,
            "gwt": gwt, "gb": gb1, "ew": expert_w, "eb": eb1,
            "ones_bf": ones_bf, "onescol_bf": onescol_bf, "onesmat_bf": onesmat_bf1,
            "ones_f": ones_f, "epscol": epscol,
        })

    res = run_bass_kernel_spmd(nc, in_maps, core_ids=list(range(8)))
    out = np.empty((4, S, D), dtype=np.float32)
    for core in range(8):
        b, hf = core // 2, core % 2
        out[b, hf * NQ:(hf + 1) * NQ] = res.results[core]["y"]
    return out


if __name__ == "__main__":
    build_program()
    print("build ok")


# version 03
# speedup vs baseline: 1.0264x; 1.0264x over previous
import numpy as np
import ml_dtypes
from contextlib import ExitStack

import concourse.bass as bass
import concourse.mybir as mybir
from concourse.bass_utils import run_bass_kernel_spmd

F32 = mybir.dt.float32
BF16 = mybir.dt.bfloat16
AF = mybir.ActivationFunctionType
ALU = mybir.AluOpType
AX = mybir.AxisListType

D = 2048
S = 4096
NQ = 2048
NH = 16
HD = 128
G = 2
CC = D // 128
KC = S // 128
QTW = 512
NQT = NQ // QTW
STRIP = 512
NSTRIP = S // STRIP
NE = 8
SCALE = float(HD) ** -0.5
EPS = 1e-5
NTT = NQ // 128

ENGINES = ("tensor", "vector", "scalar", "sync", "gpsimd")


class Sem:
    def __init__(self, ctx, nc, name):
        self.h = ctx.enter_context(nc.semaphore(name))
        self.n = 0
        self.name = name


class Sched:
    def __init__(self):
        self.q = {e: [] for e in ENGINES}

    def op(self, eng, fn, waits=(), incs=()):
        ev = None
        inc_list = []
        for sem, amt in incs:
            sem.n += amt
            inc_list.append((sem, amt))
            if ev is None:
                ev = (sem, sem.n)
        self.q[eng].append((tuple(w for w in waits if w is not None), fn, tuple(inc_list)))
        return ev

    def emit(self, blk):
        for eng in ENGINES:
            items = self.q[eng]
            if not items:
                continue

            def body(engine, items=items):
                for waits, fn, incs in items:
                    for sem, val in waits:
                        engine.wait_ge(sem.h, val)
                    inst = fn(engine)
                    for sem, amt in incs:
                        inst.then_inc(sem.h, amt)

            getattr(blk, eng)(body)


def build_program(npass=8, do_bc=True, dbg_attn=False, dbg_x2=False, dbg_proj=False,
                  dbg_att1=False):
    nc = bass.Bass("TRN2", target_bir_lowering=False)

    xT = nc.dram_tensor("xT", [D, S], F32, kind="ExternalInput").ap()
    xq = nc.dram_tensor("xq", [NQ, D], F32, kind="ExternalInput").ap()
    wqT = nc.dram_tensor("wqT", [D, D], F32, kind="ExternalInput").ap()
    wkT = nc.dram_tensor("wkT", [D, D], F32, kind="ExternalInput").ap()
    wvT = nc.dram_tensor("wvT", [D, D], F32, kind="ExternalInput").ap()
    woT = nc.dram_tensor("woT", [D, D], F32, kind="ExternalInput").ap()
    gwt = nc.dram_tensor("gwt", [128, CC * NE], F32, kind="ExternalInput").ap()
    gb = nc.dram_tensor("gb", [1, NE], F32, kind="ExternalInput").ap()
    ew = nc.dram_tensor("ew", [NE, D, D], F32, kind="ExternalInput").ap()
    eb = nc.dram_tensor("eb", [1, NE * D], F32, kind="ExternalInput").ap()
    ones_bf = nc.dram_tensor("ones_bf", [1, 128], BF16, kind="ExternalInput").ap()
    onescol_bf = nc.dram_tensor("onescol_bf", [128, 1], BF16, kind="ExternalInput").ap()
    onesmat_bf = nc.dram_tensor("onesmat_bf", [128, 128], BF16, kind="ExternalInput").ap()
    ones_f = nc.dram_tensor("ones_f", [1, 128], F32, kind="ExternalInput").ap()
    epscol = nc.dram_tensor("epscol", [128, 1], F32, kind="ExternalInput").ap()

    y = nc.dram_tensor("y", [NQ, D], F32, kind="ExternalOutput").ap()
    dbgA = nc.dram_tensor("dbgA", [128, NH * NQ], F32, kind="ExternalOutput").ap() if dbg_attn else None
    dbgX = nc.dram_tensor("dbgX", [NQ, D], F32, kind="ExternalOutput").ap() if dbg_x2 else None
    if dbg_proj:
        dbgK = nc.dram_tensor("dbgK", [128, G * S], F32, kind="ExternalOutput").ap()
        dbgQ = nc.dram_tensor("dbgQ", [128, G * NQ], F32, kind="ExternalOutput").ap()
        dbgV = nc.dram_tensor("dbgV", [128, KC * G * HD], F32, kind="ExternalOutput").ap()
        dbgN = nc.dram_tensor("dbgN", [128, 16], F32, kind="ExternalOutput").ap()

    if dbg_att1:
        dbgD = nc.dram_tensor("dbgD", [1, 512], F32, kind="ExternalOutput").ap()
        dbgR = nc.dram_tensor("dbgR", [128, 512], F32, kind="ExternalOutput").ap()
        dbgE = nc.dram_tensor("dbgE", [128, 512], F32, kind="ExternalOutput").ap()
        dbgE0 = nc.dram_tensor("dbgE0", [128, 512], F32, kind="ExternalOutput").ap()
        dbgE1 = nc.dram_tensor("dbgE1", [128, 512], F32, kind="ExternalOutput").ap()
        dbgU = nc.dram_tensor("dbgU", [128, 512], F32, kind="ExternalOutput").ap()

    x2_scr = nc.dram_tensor("x2_scr", [NQ, D], F32, kind="Internal").ap()
    x2bf_scr = nc.dram_tensor("x2bf_scr", [NQ, D], BF16, kind="Internal").ap()

    ctx = ExitStack()

    onesbf_sb = ctx.enter_context(nc.sbuf_tensor("onesbf_sb", [1, 128], BF16, side="right"))
    onescol_sb = ctx.enter_context(nc.sbuf_tensor("onescol_sb", [128, 1], BF16, side="right"))
    onesmat_sb = ctx.enter_context(nc.sbuf_tensor("onesmat_sb", [128, 128], BF16, side="right"))
    onesf_sb = ctx.enter_context(nc.sbuf_tensor("onesf_sb", [1, 128], F32, side="right"))
    eps_sb = ctx.enter_context(nc.sbuf_tensor("eps_sb", [128, 1], F32, side="right"))

    ps = [ctx.enter_context(nc.psum_tensor(f"ps{i}", [128, 512], F32)) for i in range(4)]
    ps_sc = [ctx.enter_context(nc.psum_tensor(f"ps_sc{i}", [128, 512], F32)) for i in range(2)]
    ps_at = ctx.enter_context(nc.psum_tensor("ps_at", [128, 512], F32))
    ps_dn = ctx.enter_context(nc.psum_tensor("ps_dn", [128, 512], F32))

    s_xt = Sem(ctx, nc, "s_xt")
    s_w = Sem(ctx, nc, "s_w")
    s_pe = Sem(ctx, nc, "s_pe")
    s_sc = Sem(ctx, nc, "s_sc")
    s_ve = Sem(ctx, nc, "s_ve")
    s_dq = Sem(ctx, nc, "s_dq")
    s_sy = Sem(ctx, nc, "s_sy")
    s_out = Sem(ctx, nc, "s_out")

    sch = Sched()
    blk = ctx.enter_context(nc.Block())

    bank_free = {}

    def wfree(bid):
        return bank_free.get(bid)

    with ctx:
        st_ab = ExitStack()
        attnT = st_ab.enter_context(nc.sbuf_tensor("attnT", [128, NH * NQ], BF16))
        st_a = ExitStack()

        def sba(name, shape, dt):
            return st_a.enter_context(nc.sbuf_tensor(name, shape, dt))

        wq_b = [sba(f"wqb{i}", [128, CC * G * HD], BF16) for i in range(2)]
        wk_b = [sba(f"wkb{i}", [128, CC * G * HD], BF16) for i in range(2)]
        wv_b = [sba(f"wvb{i}", [128, CC * G * HD], BF16) for i in range(2)]
        xt_b = [sba(f"xtb{i}", [128, CC * STRIP], BF16) for i in range(2)]
        KT = sba("KT", [128, G * S], BF16)
        QTm = sba("QTm", [128, G * NQ], BF16)
        Vm = sba("Vm", [128, KC * G * HD], BF16)
        nsc = sba("nsc", [128, 2 * 512], BF16)
        runmx = sba("runmx", [1, 8], F32)
        sc11 = sba("sc11", [1, 8], F32)
        negc = sba("negc", [1, G], F32)
        negc_col = sba("negc_col", [128, G], F32)
        expS = [sba(f"expS{i}", [128, 512], BF16) for i in range(2)]
        dnln = sba("dnln", [128, 512], F32)
        recip_rep = sba("recip_rep", [128, 512], F32)

        for t_sb, t_dr in ((onesbf_sb, ones_bf), (onescol_sb, onescol_bf),
                           (onesmat_sb, onesmat_bf), (onesf_sb, ones_f), (eps_sb, epscol)):
            sch.op("gpsimd", lambda g, a=t_sb, b=t_dr: g.dma_start(a[:, :], b),
                   incs=[(s_w, 16)])

        def w_slice_dma(dst, srcT, p, waits=()):
            src = srcT.rearrange("(c p) d -> p c d", p=128)[:, :, p * G * HD:(p + 1) * G * HD]
            d3 = dst[:, :].rearrange("p (c n) -> p c n", c=CC)
            ev = None
            for half in range(2):
                cs = slice(half * 8, (half + 1) * 8)
                ev = sch.op("gpsimd",
                            lambda g, d=d3[:, cs, :], s2=src[:, cs, :]: g.dma_start(d, s2),
                            waits=waits, incs=[(s_w, 16)])
            return ev

        w_slice_dma(wq_b[0], wqT, 0)
        w_slice_dma(wk_b[0], wkT, 0)
        w_ready = [w_slice_dma(wv_b[0], wvT, 0)]

        xT_r = xT.rearrange("(c p) t -> p c t", p=128)

        strip_ev = {}
        strip_done = {}
        pass_proj_last = {}
        kq_evac_last = [None]
        attn_ev = []
        exp_free = {}
        expS_free = {}
        dump_gate = {}
        gchunk = [0]
        d_ev_last = [None]

        for p in range(npass):
            wq, wk, wv = wq_b[p % 2], wk_b[p % 2], wv_b[p % 2]
            grp_ctr = [0]
            norm_ctr = [0]
            nsc_free = {}

            mx_init_ev = sch.op("vector", lambda v: v.memset(runmx[:, 0:4], -1e30),
                                waits=[attn_ev[-1]] if attn_ev else (), incs=[(s_ve, 1)])
            norm_last = {0: mx_init_ev, 1: mx_init_ev, 2: mx_init_ev, 3: mx_init_ev}

            def proj_group(p, s, lhs_fn, rhs_fn, n_out, dst, doff, evac_eng):
                g_i = grp_ctr[0] % 4
                grp_ctr[0] += 1
                waits = [strip_ev[(p, s)], wfree(g_i), w_ready[p]]

                def fn(t, bank=ps[g_i], lhs_fn=lhs_fn, rhs_fn=rhs_fn, n_out=n_out):
                    inst = None
                    for c in range(CC):
                        inst = t.matmul(bank[:, 0:n_out], lhs_fn(c), rhs_fn(c),
                                        start=(c == 0), stop=(c == CC - 1))
                    return inst

                mm_ev = sch.op("tensor", fn, waits=waits, incs=[(s_pe, 1)])
                strip_done[(p, s)] = s_pe.n
                sem = s_ve if evac_eng == "vector" else s_sc

                def ev_fn(e, dst=dst, doff=doff, n_out=n_out, bank=ps[g_i]):
                    if evac_eng == "vector":
                        return e.tensor_copy(dst[:, doff:doff + n_out], bank[:, 0:n_out])
                    return e.copy(dst[:, doff:doff + n_out], bank[:, 0:n_out])

                evac_ev = sch.op(evac_eng, ev_fn, waits=[mm_ev], incs=[(sem, 1)])
                bank_free[g_i] = evac_ev
                return evac_ev

            def sq_and_norm(p, src, soff, col, prior_ev):
                ping = norm_ctr[0] % 2
                norm_ctr[0] += 1
                waits = [prior_ev, nsc_free.get(ping)]
                sq_ev = sch.op("scalar",
                               lambda sc, src=src, soff=soff, ping=ping: sc.activation(
                                   nsc[:, ping * 512:(ping + 1) * 512],
                                   src[:, soff:soff + 512], AF.Square),
                               waits=waits, incs=[(s_sc, 1)])
                mm_ev = sch.op("tensor",
                               lambda t, ping=ping: t.matmul(
                                   ps_dn[0:1, :], onescol_sb[:, :],
                                   nsc[:, ping * 512:(ping + 1) * 512],
                                   start=True, stop=True),
                               waits=[sq_ev, wfree("dn")], incs=[(s_pe, 1)])
                nsc_free[ping] = mm_ev
                r_ev = sch.op("vector",
                              lambda v: v.reduce_max(sc11[:, 7:8], ps_dn[0:1, :], axis=AX.X),
                              waits=[mm_ev, norm_last[col]], incs=[(s_ve, 1)])
                bank_free["dn"] = r_ev
                m_ev = sch.op("vector",
                              lambda v, col=col: v.tensor_tensor(
                                  runmx[:, col:col + 1], runmx[:, col:col + 1],
                                  sc11[:, 7:8], ALU.max),
                              waits=[r_ev], incs=[(s_ve, 1)])
                norm_last[col] = m_ev

            for s in range(NSTRIP):
                waits = []
                if (p, s) != (0, 0) and (p, s) != (0, 1):
                    pp, ss = (p, s - 2) if s >= 2 else (p - 1, NSTRIP + s - 2)
                    waits = [(s_pe, strip_done[(pp, ss)])]
                dst = xt_b[s % 2][:, :].rearrange("p (c n) -> p c n", c=CC)
                src = xT_r[:, :, s * STRIP:(s + 1) * STRIP]
                evx = None
                for half in range(2):
                    cs = slice(half * 8, (half + 1) * 8)
                    evx = sch.op("gpsimd",
                                 lambda g, d=dst[:, cs, :], sr=src[:, cs, :]: g.dma_start(d, sr),
                                 waits=waits, incs=[(s_xt, 16)])
                strip_ev[(p, s)] = evx
                xb = xt_b[s % 2]

                for dc in range(G):
                    e1 = proj_group(
                        p, s,
                        lambda c, dc=dc, wk=wk: wk[:, c * G * HD + dc * HD: c * G * HD + (dc + 1) * HD],
                        lambda c, xb=xb: xb[:, c * STRIP:(c + 1) * STRIP],
                        STRIP, KT, dc * S + s * STRIP, "vector")
                    kq_evac_last[0] = e1
                    sq_and_norm(p, KT, dc * S + s * STRIP, 2 + dc, e1)
                if s < NQ // STRIP:
                    for dc in range(G):
                        e1 = proj_group(
                            p, s,
                            lambda c, dc=dc, wq=wq: wq[:, c * G * HD + dc * HD: c * G * HD + (dc + 1) * HD],
                            lambda c, xb=xb: xb[:, c * STRIP:(c + 1) * STRIP],
                            STRIP, QTm, dc * NQ + s * STRIP, "vector")
                        kq_evac_last[0] = e1
                        sq_and_norm(p, QTm, dc * NQ + s * STRIP, dc, e1)
                for tt in range(STRIP // 128):
                    ktile = s * (STRIP // 128) + tt
                    proj_group(
                        p, s,
                        lambda c, xb=xb, tt=tt: xb[:, c * STRIP + tt * 128: c * STRIP + (tt + 1) * 128],
                        lambda c, wv=wv: wv[:, c * G * HD:(c + 1) * G * HD],
                        G * HD, Vm, ktile * G * HD, "scalar")

            pass_proj_last[p] = s_pe.n

            if p + 1 < npass:
                nxt = (p + 1) % 2
                w0 = [(s_pe, pass_proj_last[p - 1])] if p >= 1 else ()
                w_slice_dma(wq_b[nxt], wqT, p + 1, waits=w0)
                w_slice_dma(wk_b[nxt], wkT, p + 1)
                w_ready.append(w_slice_dma(wv_b[nxt], wvT, p + 1))

            neg_evs = []
            for h in range(G):
                e1 = sch.op("scalar",
                            lambda sc, h=h: sc.activation(sc11[:, 0:1], runmx[:, h:h + 1], AF.Ln),
                            waits=[norm_last[h], norm_last[2 + h]], incs=[(s_sc, 1)])
                e2 = sch.op("scalar",
                            lambda sc, h=h: sc.activation(sc11[:, 1:2], runmx[:, 2 + h:3 + h], AF.Ln),
                            waits=[e1], incs=[(s_sc, 1)])
                e3 = sch.op("vector",
                            lambda v: v.tensor_tensor(sc11[:, 2:3], sc11[:, 0:1], sc11[:, 1:2], ALU.add),
                            waits=[e2], incs=[(s_ve, 1)])
                e4 = sch.op("scalar",
                            lambda sc: sc.activation(sc11[:, 3:4], sc11[:, 2:3], AF.Exp, scale=0.5),
                            waits=[e3], incs=[(s_sc, 1)])
                e5 = sch.op("vector",
                            lambda v, h=h: v.tensor_scalar(
                                negc[:, h:h + 1], sc11[:, 3:4], -1.01 * SCALE, None, ALU.mult),
                            waits=[e4], incs=[(s_ve, 1)])
                neg_evs.append(e5)
            rep_mm = sch.op("tensor",
                            lambda t: t.matmul(ps_at[:, 0:G], onesf_sb[:, :], negc[:, 0:G],
                                               start=True, stop=True),
                            waits=neg_evs + [wfree("at")], incs=[(s_pe, 1)])
            rep_cp = sch.op("scalar",
                            lambda sc: sc.copy(negc_col[:, :], ps_at[:, 0:G]),
                            waits=[rep_mm], incs=[(s_sc, 1)])
            bank_free["at"] = rep_cp

            for h in range(1 if dbg_att1 else G):
                hd = p * G + h
                for qt in range(1 if dbg_att1 else NQT):
                    qoff = qt * QTW
                    pend = []

                    def emit_attn_chunk(kc, ping, e_ev):
                        aw = [e_ev]
                        if kc == 0:
                            aw.append(wfree("at"))

                        def mm_a(t, kc=kc, h=h, ping=ping):
                            return t.matmul(ps_at[:, :],
                                            Vm[:, kc * G * HD + h * HD: kc * G * HD + (h + 1) * HD],
                                            expS[ping][:, :],
                                            start=(kc == 0), stop=(kc == KC - 1))

                        a_ev = sch.op("tensor", mm_a, waits=aw, incs=[(s_pe, 1)])
                        dw = [wfree("dn")] if kc == 0 else []

                        def mm_d(t, kc=kc, ping=ping):
                            return t.matmul(ps_dn[:, :], onesmat_sb[:, :], expS[ping][:, :],
                                            start=(kc == 0), stop=(kc == KC - 1))

                        d_ev = sch.op("tensor", mm_d, waits=dw, incs=[(s_pe, 1)])
                        expS_free[ping] = d_ev
                        return a_ev, d_ev

                    a_ev = d_ev = None
                    for kc in range(KC):
                        ping = gchunk[0] % 2
                        gchunk[0] += 1
                        sw = [exp_free.get(ping)]
                        if kc == 0 and qt == 0 and h == 0:
                            sw.append(kq_evac_last[0])

                        def mm_s(t, h=h, kc=kc, qoff=qoff, ping=ping):
                            return t.matmul(ps_sc[ping][:, :],
                                            KT[:, h * S + kc * 128:h * S + (kc + 1) * 128],
                                            QTm[:, h * NQ + qoff:h * NQ + qoff + QTW],
                                            start=True, stop=True)

                        s_ev = sch.op("tensor", mm_s, waits=sw, incs=[(s_pe, 1)])

                        ew_ = [s_ev, expS_free.get(ping), dump_gate.pop(ping, None)]
                        if kc == 0 and qt == 0 and h == 0:
                            ew_.append(rep_cp)
                        e_ev = sch.op("scalar",
                                      lambda sc, ping=ping, h=h: sc.activation(
                                          expS[ping][:, :], ps_sc[ping][:, :], AF.Exp,
                                          bias=negc_col[:, h:h + 1], scale=SCALE),
                                      waits=ew_, incs=[(s_sc, 1)])
                        exp_free[ping] = e_ev
                        if dbg_att1 and p == 0 and h == 0 and qt == 0 and kc in (0, 16):
                            dst = dbgE0 if kc == 0 else dbgE1
                            dev_ = sch.op("gpsimd",
                                          lambda g, dst=dst, ping=ping: g.dma_start(
                                              dst, expS[ping][:, :]),
                                          waits=[e_ev], incs=[(s_out, 16)])
                            dump_gate[ping] = dev_
                        pend.append((kc, ping, e_ev))
                        if len(pend) >= 2:
                            kc0, ping0, e0 = pend.pop(0)
                            a_ev, d_ev = emit_attn_chunk(kc0, ping0, e0)
                    while pend:
                        kc0, ping0, e0 = pend.pop(0)
                        a_ev, d_ev = emit_attn_chunk(kc0, ping0, e0)

                    r1 = sch.op("scalar",
                                lambda sc: sc.copy(dnln[:, :], ps_dn[:, :]),
                                waits=[d_ev], incs=[(s_sc, 1)])
                    bank_free["dn"] = r1
                    r2w = [r1, attn_ev[-1] if attn_ev else None]
                    r2 = sch.op("vector",
                                lambda v: v.reciprocal(recip_rep[:, :], dnln[:, :]),
                                waits=r2w, incs=[(s_ve, 1)])
                    t_ev = sch.op("vector",
                                  lambda v, hd=hd, qoff=qoff: v.tensor_tensor(
                                      attnT[:, hd * NQ + qoff: hd * NQ + qoff + QTW],
                                      ps_at[:, :], recip_rep[:, :], ALU.mult),
                                  waits=[a_ev, r2], incs=[(s_ve, 1)])
                    attn_ev.append(t_ev)
                    bank_free["at"] = t_ev
                    if dbg_att1 and p == 0 and h == 0 and qt == 0:
                        sch.op("gpsimd", lambda g: g.dma_start(dbgR, recip_rep[:, :]),
                               waits=[t_ev], incs=[(s_out, 16)])
                        sch.op("gpsimd",
                               lambda g: g.dma_start(dbgE, expS[(gchunk[0] - 1) % 2][:, :]),
                               waits=[t_ev], incs=[(s_out, 16)])

        phA = [(s_pe, s_pe.n), (s_ve, s_ve.n), (s_sc, s_sc.n)]

        if dbg_proj:
            wv_ = [(s_pe, s_pe.n), (s_ve, s_ve.n), (s_sc, s_sc.n)]
            sch.op("gpsimd", lambda g: g.dma_start(dbgK, KT[:, :]), waits=wv_, incs=[(s_out, 16)])
            sch.op("gpsimd", lambda g: g.dma_start(dbgQ, QTm[:, :]), incs=[(s_out, 16)])
            sch.op("gpsimd", lambda g: g.dma_start(dbgV, Vm[:, :]), incs=[(s_out, 16)])
            sch.op("gpsimd", lambda g: g.dma_start(dbgN[:, 0:G], negc_col[:, :]), incs=[(s_out, 16)])
            sch.op("gpsimd", lambda g: g.dma_start(dbgN[0:1, 8:16], runmx[:, :]), incs=[(s_out, 16)])

        if dbg_attn:
            dmp = sch.op("gpsimd", lambda g: g.dma_start(dbgA, attnT[:, :]),
                         waits=[attn_ev[-1]], incs=[(s_out, 16)])
            sch.op("sync", lambda s2: s2.nop(), waits=[(s_out, s_out.n)])

        if not do_bc:
            sch.emit(blk)
            st_a.close()
            st_ab.close()
            return nc

        st_a.close()
        X2T = ctx.enter_context(nc.sbuf_tensor("X2T", [128, CC * 1024], BF16, side="right"))
        x2bf = [ctx.enter_context(nc.sbuf_tensor(f"x2bf{i}", [128, D], BF16, side="right"))
                for i in range(2)]
        st_b = ExitStack()
        woT_b = st_b.enter_context(nc.sbuf_tensor("woT_b", [128, CC * D], BF16))
        xres = [st_b.enter_context(nc.sbuf_tensor(f"xres{i}", [128, D], F32)) for i in range(2)]
        cen = [st_b.enter_context(nc.sbuf_tensor(f"cen{i}", [128, D], F32)) for i in range(2)]
        nmean = st_b.enter_context(nc.sbuf_tensor("nmean", [128, 1], F32))
        sumsq = st_b.enter_context(nc.sbuf_tensor("sumsq", [128, 1], F32))
        rstd = st_b.enter_context(nc.sbuf_tensor("rstd", [128, 1], F32))
        sqd = st_b.enter_context(nc.sbuf_tensor("sqd", [128, D], BF16))

        woT_ev = []
        for hdc in range(CC):
            wv_ = phA if hdc == 0 else ()
            woT_ev.append(sch.op("gpsimd",
                                 lambda g, hdc=hdc: g.dma_start(
                                     woT_b[:, hdc * D:(hdc + 1) * D],
                                     woT[hdc * 128:(hdc + 1) * 128, :]),
                                 waits=wv_, incs=[(s_w, 16)]))

        resid_ev = {}
        cen_free = {}
        x2w_ev = {}
        x2bf_free = {}
        tposed_ev = {}
        ln_chain = [None]

        for tt in range(NTT):
            pi = tt % 2
            xr_ev = sch.op("scalar",
                           lambda sc, tt=tt, pi=pi: sc.dma_start(
                               xres[pi][:, :], xq[tt * 128:(tt + 1) * 128, :]),
                           waits=[resid_ev.get(tt - 2)], incs=[(s_dq, 16)])
            g_evs = []
            for dmt in range(4):
                waits = [wfree(dmt)]
                if tt == 0:
                    waits += [woT_ev[-1]] + phA

                def fn(t, dmt=dmt, tt=tt):
                    inst = None
                    for hdc in range(CC):
                        inst = t.matmul(ps[dmt][:, :],
                                        attnT[:, hdc * NQ + tt * 128:hdc * NQ + (tt + 1) * 128],
                                        woT_b[:, hdc * D + dmt * 512:hdc * D + (dmt + 1) * 512],
                                        start=(hdc == 0), stop=(hdc == CC - 1))
                    return inst

                g_evs.append(sch.op("tensor", fn, waits=waits, incs=[(s_pe, 1)]))
            r_evs = []
            for dmt in range(4):
                r_evs.append(sch.op("vector",
                                    lambda v, dmt=dmt, pi=pi: v.tensor_tensor(
                                        cen[pi][:, dmt * 512:(dmt + 1) * 512],
                                        ps[dmt][:, :],
                                        xres[pi][:, dmt * 512:(dmt + 1) * 512], ALU.add),
                                    waits=[g_evs[dmt], xr_ev, cen_free.get(tt - 2)],
                                    incs=[(s_ve, 1)]))
                bank_free[dmt] = r_evs[-1]
            resid_ev[tt] = r_evs[-1]

            l1 = sch.op("vector",
                        lambda v, pi=pi: v.reduce_sum(nmean[:, :], cen[pi][:, :], axis=AX.X, negate=True),
                        waits=r_evs + [ln_chain[0]], incs=[(s_ve, 1)])
            l2 = sch.op("vector",
                        lambda v: v.tensor_scalar(nmean[:, :], nmean[:, :], 1.0 / D, None, ALU.mult),
                        waits=[l1], incs=[(s_ve, 1)])
            l3 = sch.op("vector",
                        lambda v, pi=pi: v.tensor_scalar(cen[pi][:, :], cen[pi][:, :],
                                                         nmean[:, :], None, ALU.add),
                        waits=[l2], incs=[(s_ve, 1)])
            l4 = sch.op("scalar",
                        lambda sc, pi=pi: sc.activation(sqd[:, :], cen[pi][:, :], AF.Square,
                                                        accum_out=sumsq[:, :]),
                        waits=[l3], incs=[(s_sc, 1)])
            l5 = sch.op("scalar",
                        lambda sc: sc.activation(sumsq[:, :], sumsq[:, :], AF.Ln,
                                                 bias=eps_sb[:, 0:1], scale=1.0 / D),
                        waits=[l4], incs=[(s_sc, 1)])
            l6 = sch.op("scalar",
                        lambda sc: sc.activation(rstd[:, :], sumsq[:, :], AF.Exp, scale=-0.5),
                        waits=[l5], incs=[(s_sc, 1)])
            l7 = sch.op("vector",
                        lambda v, pi=pi: v.tensor_scalar(cen[pi][:, :], cen[pi][:, :],
                                                         rstd[:, :], None, ALU.mult),
                        waits=[l6], incs=[(s_ve, 1)])
            l8 = sch.op("vector",
                        lambda v, pi=pi: v.tensor_copy(x2bf[pi][:, :], cen[pi][:, :]),
                        waits=[l7, x2bf_free.get(tt - 2)], incs=[(s_ve, 1)])
            ln_chain[0] = l8

            w1 = sch.op("scalar",
                        lambda sc, tt=tt, pi=pi: sc.dma_start(
                            x2_scr[tt * 128:(tt + 1) * 128, :], cen[pi][:, :]),
                        waits=[l7], incs=[(s_dq, 16)])
            x2w_ev[tt] = w1
            cen_free[tt] = w1
            if tt >= 8:
                w2 = sch.op("scalar",
                            lambda sc, tt=tt, pi=pi: sc.dma_start(
                                x2bf_scr[tt * 128:(tt + 1) * 128, :], x2bf[pi][:, :]),
                            waits=[l8], incs=[(s_dq, 16)])
                x2bf_free[tt] = w2
            else:
                tp = None
                for c in range(CC):
                    tp = sch.op("sync",
                                lambda s2, c=c, tt=tt, pi=pi: s2.dma_start(
                                    X2T[:, c * 1024 + tt * 128:c * 1024 + (tt + 1) * 128],
                                    x2bf[pi][:, c * 128:(c + 1) * 128], transpose=True),
                                waits=[l8] if c == 0 else (), incs=[(s_sy, 16)])
                tposed_ev[tt] = tp
                x2bf_free[tt] = tp

            if dbg_x2:
                sch.op("gpsimd",
                       lambda g, tt=tt, pi=pi: g.dma_start(
                           dbgX[tt * 128:(tt + 1) * 128, :], cen[pi][:, :]),
                       waits=[l7], incs=[(s_out, 16)])

        phB = [(s_pe, s_pe.n), (s_ve, s_ve.n), (s_sc, s_sc.n), (s_dq, s_dq.n), (s_sy, s_sy.n)]

        st_b.close()
        st_ab.close()
        st_c = ExitStack()

        def sbc(name, shape, dt):
            return st_c.enter_context(nc.sbuf_tensor(name, shape, dt))

        acc = sbc("acc", [128, 8 * D], F32)
        We_b = [sbc(f"Web{i}", [128, CC * 512], BF16) for i in range(2)]
        tmp = [sbc(f"tmp{i}", [128, 512], F32) for i in range(2)]
        yb = [sbc(f"yb{i}", [128, D], F32) for i in range(2)]
        gwb = sbc("gwb", [128, CC * NE], BF16)
        gbb = sbc("gbb", [1, NE], BF16)
        ebb = [sbc(f"ebb{i}", [1, D], BF16) for i in range(2)]
        zg = sbc("zg", [128, NE], F32)
        zz = sbc("zz", [128, NE], F32)
        m1 = sbc("m1", [128, 1], F32)
        m2 = sbc("m2", [128, 1], F32)
        mask = sbc("mask", [128, 8 * NE], F32)
        nmean2 = sbc("nmean2", [128, 1], F32)
        sumsq2 = sbc("sumsq2", [128, 1], F32)
        rstd2 = sbc("rstd2", [128, 1], F32)
        sqd2 = sbc("sqd2", [128, D], BF16)

        gw_ev = sch.op("gpsimd", lambda g: g.dma_start(gwb[:, :], gwt), waits=phB, incs=[(s_w, 16)])
        gb_ev = sch.op("gpsimd", lambda g: g.dma_start(gbb[:, :], gb), incs=[(s_w, 16)])

        grp_c = [0]
        acc_w = {}
        tmp_free = {}
        We_free = {}
        ebb_free = {}
        yb_free = {}
        acc_init_ev = {}
        ln2_chain = [None]
        halfC_pe_done = [None]
        tpose_c = {}
        x2bf_ld = {}
        last_y = []

        for H in range(2):
            if H == 1:
                for ti in range(8):
                    tt = 8 + ti
                    pi = tt % 2
                    ld = sch.op("scalar",
                                lambda sc, tt=tt, pi=pi: sc.dma_start(
                                    x2bf[pi][:, :], x2bf_scr[tt * 128:(tt + 1) * 128, :]),
                                waits=[x2bf_free.get(tt - 2), x2w_ev.get(tt),
                                       (s_dq, phB[3][1])], incs=[(s_dq, 16)])
                    tp = None
                    for c in range(CC):
                        wv_ = [ld]
                        if ti == 0 and c == 0:
                            wv_.append((s_pe, halfC_pe_done[0]))
                        tp = sch.op("sync",
                                    lambda s2, c=c, ti=ti, pi=pi: s2.dma_start(
                                        X2T[:, c * 1024 + ti * 128:c * 1024 + (ti + 1) * 128],
                                        x2bf[pi][:, c * 128:(c + 1) * 128], transpose=True),
                                    waits=wv_, incs=[(s_sy, 16)])
                    tpose_c[tt] = tp
                    x2bf_free[tt] = tp

            for ti in range(8):
                tt = H * 8 + ti
                wv_ = [x2w_ev[tt], acc_w.get(("ln2", ti))]
                if H == 0 and ti == 0:
                    wv_ += phB
                acc_init_ev[ti] = sch.op("scalar",
                                         lambda sc, tt=tt, ti=ti: sc.dma_start(
                                             acc[:, ti * D:(ti + 1) * D],
                                             x2_scr[tt * 128:(tt + 1) * 128, :]),
                                         waits=wv_, incs=[(s_dq, 16)])

            mask_ev = {}
            for ti in range(8):
                tt = H * 8 + ti
                g_i = grp_c[0] % 4
                grp_c[0] += 1
                wv_ = [gw_ev, gb_ev, wfree(g_i),
                       tposed_ev.get(tt) if H == 0 else tpose_c.get(tt)]

                def fng(t, ti=ti, g_i=g_i):
                    t.matmul(ps[g_i][:, 0:NE], onesbf_sb[:, :], gbb[:, :], start=True, stop=False)
                    inst = None
                    for c in range(CC):
                        inst = t.matmul(ps[g_i][:, 0:NE],
                                        X2T[:, c * 1024 + ti * 128:c * 1024 + (ti + 1) * 128],
                                        gwb[:, c * NE:(c + 1) * NE],
                                        start=False, stop=(c == CC - 1))
                    return inst

                gm = sch.op("tensor", fng, waits=wv_, incs=[(s_pe, 1)])
                zev = sch.op("scalar", lambda sc, g_i=g_i: sc.copy(zg[:, :], ps[g_i][:, 0:NE]),
                             waits=[gm, mask_ev.get(ti - 1)], incs=[(s_sc, 1)])
                bank_free[g_i] = zev
                c1 = sch.op("vector", lambda v: v.reduce_max(m1[:, :], zg[:, :], axis=AX.X),
                            waits=[zev], incs=[(s_ve, 1)])
                c2 = sch.op("vector",
                            lambda v: v.tensor_scalar(zz[:, :], zg[:, :], m1[:, :], None, ALU.is_equal),
                            waits=[c1], incs=[(s_ve, 1)])
                c3 = sch.op("vector",
                            lambda v: v.scalar_tensor_tensor(zz[:, :], zz[:, :], -1e30, zg[:, :],
                                                             ALU.mult, ALU.add),
                            waits=[c2], incs=[(s_ve, 1)])
                c4 = sch.op("vector", lambda v: v.reduce_max(m2[:, :], zz[:, :], axis=AX.X),
                            waits=[c3], incs=[(s_ve, 1)])
                c5 = sch.op("vector",
                            lambda v, ti=ti: v.tensor_scalar(mask[:, ti * NE:(ti + 1) * NE],
                                                             zg[:, :], m2[:, :], 0.5,
                                                             ALU.is_ge, ALU.mult),
                            waits=[c4], incs=[(s_ve, 1)])
                mask_ev[ti] = c5

            for e in range(NE):
                wpi = e % 2
                eb_ev = sch.op("gpsimd",
                               lambda g, e=e, wpi=wpi: g.dma_start(
                                   ebb[wpi][:, :], eb[0:1, e * D:(e + 1) * D]),
                               waits=[ebb_free.get(wpi)], incs=[(s_w, 16)])
                for oq in range(4):
                    bpi = (e * 4 + oq) % 2
                    src = ew[e].rearrange("(c p) o -> p c o", p=128)[:, :, oq * 512:(oq + 1) * 512]
                    d3 = We_b[bpi][:, :].rearrange("p (c n) -> p c n", c=CC)
                    we_ev = None
                    for half in range(2):
                        cs = slice(half * 8, (half + 1) * 8)
                        we_ev = sch.op("gpsimd",
                                       lambda g, d=d3[:, cs, :], s2=src[:, cs, :]: g.dma_start(d, s2),
                                       waits=[We_free.get(bpi)],
                                       incs=[(s_w, 16)])
                    for ti in range(8):
                        g_i = grp_c[0] % 4
                        grp_c[0] += 1
                        wv_ = [we_ev, eb_ev, wfree(g_i)]

                        def fne(t, ti=ti, g_i=g_i, e=e, oq=oq, wpi=wpi, bpi=bpi):
                            t.matmul(ps[g_i][:, :], onesbf_sb[:, :],
                                     ebb[wpi][:, oq * 512:(oq + 1) * 512], start=True, stop=False)
                            inst = None
                            for c in range(CC):
                                inst = t.matmul(ps[g_i][:, :],
                                                X2T[:, c * 1024 + ti * 128:c * 1024 + (ti + 1) * 128],
                                                We_b[bpi][:, c * 512:(c + 1) * 512],
                                                start=False, stop=(c == CC - 1))
                            return inst

                        hm = sch.op("tensor", fne, waits=wv_, incs=[(s_pe, 1)])
                        if ti == 7:
                            We_free[bpi] = hm
                            if oq == 3:
                                ebb_free[wpi] = hm
                        tpi = grp_c[0] % 2
                        rl = sch.op("scalar",
                                    lambda sc, g_i=g_i, ti=ti, e=e, tpi=tpi: sc.activation(
                                        tmp[tpi][:, :], ps[g_i][:, :], AF.Relu,
                                        scale=mask[:, ti * NE + e:ti * NE + e + 1]),
                                    waits=[hm, mask_ev[ti], tmp_free.get(tpi)],
                                    incs=[(s_sc, 1)])
                        bank_free[g_i] = rl
                        ac = sch.op("vector",
                                    lambda v, ti=ti, oq=oq, tpi=tpi: v.tensor_tensor(
                                        acc[:, ti * D + oq * 512:ti * D + (oq + 1) * 512],
                                        acc[:, ti * D + oq * 512:ti * D + (oq + 1) * 512],
                                        tmp[tpi][:, :], ALU.add),
                                    waits=[rl, acc_w.get((ti, oq), acc_init_ev[ti])],
                                    incs=[(s_ve, 1)])
                        acc_w[(ti, oq)] = ac
                        tmp_free[tpi] = ac
            halfC_pe_done[0] = s_pe.n

            for ti in range(8):
                tt = H * 8 + ti
                pi = ti % 2
                aw = [acc_w[(ti, oq)] for oq in range(4)]
                l1 = sch.op("vector",
                            lambda v, ti=ti: v.reduce_sum(nmean2[:, :], acc[:, ti * D:(ti + 1) * D],
                                                          axis=AX.X, negate=True),
                            waits=aw + [ln2_chain[0]], incs=[(s_ve, 1)])
                l2 = sch.op("vector",
                            lambda v: v.tensor_scalar(nmean2[:, :], nmean2[:, :], 1.0 / D, None, ALU.mult),
                            waits=[l1], incs=[(s_ve, 1)])
                l3 = sch.op("vector",
                            lambda v, ti=ti: v.tensor_scalar(acc[:, ti * D:(ti + 1) * D],
                                                             acc[:, ti * D:(ti + 1) * D],
                                                             nmean2[:, :], None, ALU.add),
                            waits=[l2], incs=[(s_ve, 1)])
                l4 = sch.op("scalar",
                            lambda sc, ti=ti: sc.activation(sqd2[:, :], acc[:, ti * D:(ti + 1) * D],
                                                            AF.Square, accum_out=sumsq2[:, :]),
                            waits=[l3], incs=[(s_sc, 1)])
                l5 = sch.op("scalar",
                            lambda sc: sc.activation(sumsq2[:, :], sumsq2[:, :], AF.Ln,
                                                     bias=eps_sb[:, 0:1], scale=1.0 / D),
                            waits=[l4], incs=[(s_sc, 1)])
                l6 = sch.op("scalar",
                            lambda sc: sc.activation(rstd2[:, :], sumsq2[:, :], AF.Exp, scale=-0.5),
                            waits=[l5], incs=[(s_sc, 1)])
                l7 = sch.op("vector",
                            lambda v, ti=ti, pi=pi: v.tensor_scalar(
                                yb[pi][:, :], acc[:, ti * D:(ti + 1) * D],
                                rstd2[:, :], None, ALU.mult),
                            waits=[l6, yb_free.get(pi)], incs=[(s_ve, 1)])
                ln2_chain[0] = l7
                acc_w[("ln2", ti)] = l7
                yw = sch.op("scalar",
                            lambda sc, tt=tt, pi=pi: sc.dma_start(
                                y[tt * 128:(tt + 1) * 128, :], yb[pi][:, :]),
                            waits=[l7], incs=[(s_out, 16)])
                yb_free[pi] = yw
                last_y.append(yw)

        sch.op("sync", lambda s2: s2.nop(), waits=[(s_out, s_out.n)])

        sch.emit(blk)
        st_c.close()
    return nc


_NC_CACHE = {}


def _get_nc():
    if "nc" not in _NC_CACHE:
        _NC_CACHE["nc"] = build_program()
    return _NC_CACHE["nc"]


def kernel(L0_attn_hidden_states, L0_attn_Wq, L0_attn_Wk, L0_attn_Wv, L0_attn_Wo,
           L0_ffn_gate_w, L0_ffn_gate_b, L0_ffn_expert_w, L0_ffn_expert_b):
    x = np.ascontiguousarray(np.asarray(L0_attn_hidden_states, dtype=np.float32))
    Wq = np.asarray(L0_attn_Wq, dtype=np.float32)
    Wk = np.asarray(L0_attn_Wk, dtype=np.float32)
    Wv = np.asarray(L0_attn_Wv, dtype=np.float32)
    Wo = np.asarray(L0_attn_Wo, dtype=np.float32)
    gate_w = np.asarray(L0_ffn_gate_w, dtype=np.float32)
    gate_b = np.asarray(L0_ffn_gate_b, dtype=np.float32)
    expert_w = np.ascontiguousarray(np.asarray(L0_ffn_expert_w, dtype=np.float32))
    expert_b = np.asarray(L0_ffn_expert_b, dtype=np.float32)

    wqT = np.ascontiguousarray(Wq.T)
    wkT = np.ascontiguousarray(Wk.T)
    wvT = np.ascontiguousarray(Wv.T)
    woT = np.ascontiguousarray(Wo.T)
    gwt = np.ascontiguousarray(gate_w.reshape(CC, 128, NE).transpose(1, 0, 2).reshape(128, CC * NE))
    gb1 = np.ascontiguousarray(gate_b.reshape(1, NE))
    eb1 = np.ascontiguousarray(expert_b.reshape(1, NE * D))
    ones_bf = np.ones((1, 128), dtype=ml_dtypes.bfloat16)
    onescol_bf = np.ones((128, 1), dtype=ml_dtypes.bfloat16)
    onesmat_bf1 = np.ones((128, 128), dtype=ml_dtypes.bfloat16)
    ones_f = np.ones((1, 128), dtype=np.float32)
    epscol = np.full((128, 1), EPS, dtype=np.float32)

    nc = _get_nc()
    in_maps = []
    for core in range(8):
        b, hf = core // 2, core % 2
        own = x[b, hf * NQ:(hf + 1) * NQ]
        oth = x[b, (1 - hf) * NQ:(2 - hf) * NQ]
        xroll = np.concatenate([own, oth], axis=0)
        in_maps.append({
            "xT": np.ascontiguousarray(xroll.T),
            "xq": np.ascontiguousarray(own),
            "wqT": wqT, "wkT": wkT, "wvT": wvT, "woT": woT,
            "gwt": gwt, "gb": gb1, "ew": expert_w, "eb": eb1,
            "ones_bf": ones_bf, "onescol_bf": onescol_bf, "onesmat_bf": onesmat_bf1,
            "ones_f": ones_f, "epscol": epscol,
        })

    res = run_bass_kernel_spmd(nc, in_maps, core_ids=list(range(8)))
    out = np.empty((4, S, D), dtype=np.float32)
    for core in range(8):
        b, hf = core // 2, core % 2
        out[b, hf * NQ:(hf + 1) * NQ] = res.results[core]["y"]
    return out


if __name__ == "__main__":
    build_program()
    print("build ok")


# version 05
# speedup vs baseline: 1.0704x; 1.0429x over previous
import numpy as np
import ml_dtypes
from contextlib import ExitStack

import concourse.bass as bass
import concourse.mybir as mybir
from concourse.bass_utils import run_bass_kernel_spmd

F32 = mybir.dt.float32
BF16 = mybir.dt.bfloat16
AF = mybir.ActivationFunctionType
ALU = mybir.AluOpType
AX = mybir.AxisListType

D = 2048
S = 4096
NQ = 2048
NH = 16
HD = 128
G = 2
CC = D // 128
KC = S // 128
QTW = 512
NQT = NQ // QTW
STRIP = 512
NSTRIP = S // STRIP
NE = 8
SCALE = float(HD) ** -0.5
EPS = 1e-5
NTT = NQ // 128

ENGINES = ("tensor", "vector", "scalar", "sync", "gpsimd")


class Sem:
    def __init__(self, ctx, nc, name):
        self.h = ctx.enter_context(nc.semaphore(name))
        self.n = 0
        self.name = name


class Sched:
    def __init__(self):
        self.q = {e: [] for e in ENGINES}

    def op(self, eng, fn, waits=(), incs=()):
        ev = None
        inc_list = []
        for sem, amt in incs:
            sem.n += amt
            inc_list.append((sem, amt))
            if ev is None:
                ev = (sem, sem.n)
        self.q[eng].append((tuple(w for w in waits if w is not None), fn, tuple(inc_list)))
        return ev

    def emit(self, blk):
        for eng in ENGINES:
            items = self.q[eng]
            if not items:
                continue

            def body(engine, items=items):
                for waits, fn, incs in items:
                    for sem, val in waits:
                        engine.wait_ge(sem.h, val)
                    inst = fn(engine)
                    for sem, amt in incs:
                        inst.then_inc(sem.h, amt)

            getattr(blk, eng)(body)


def build_program(npass=8, do_bc=True, dbg_attn=False, dbg_x2=False, dbg_proj=False,
                  dbg_att1=False):
    nc = bass.Bass("TRN2", target_bir_lowering=False)

    xT = nc.dram_tensor("xT", [D, S], F32, kind="ExternalInput").ap()
    xq = nc.dram_tensor("xq", [NQ, D], F32, kind="ExternalInput").ap()
    wqT = nc.dram_tensor("wqT", [D, D], F32, kind="ExternalInput").ap()
    wkT = nc.dram_tensor("wkT", [D, D], F32, kind="ExternalInput").ap()
    wvT = nc.dram_tensor("wvT", [D, D], F32, kind="ExternalInput").ap()
    woT = nc.dram_tensor("woT", [D, D], F32, kind="ExternalInput").ap()
    gwt = nc.dram_tensor("gwt", [128, CC * NE], F32, kind="ExternalInput").ap()
    gb = nc.dram_tensor("gb", [1, NE], F32, kind="ExternalInput").ap()
    ew = nc.dram_tensor("ew", [NE, D, D], F32, kind="ExternalInput").ap()
    eb = nc.dram_tensor("eb", [1, NE * D], F32, kind="ExternalInput").ap()
    ones_bf = nc.dram_tensor("ones_bf", [1, 128], BF16, kind="ExternalInput").ap()
    onescol_bf = nc.dram_tensor("onescol_bf", [128, 1], BF16, kind="ExternalInput").ap()
    onesmat_bf = nc.dram_tensor("onesmat_bf", [128, 128], BF16, kind="ExternalInput").ap()
    ones_f = nc.dram_tensor("ones_f", [1, 128], F32, kind="ExternalInput").ap()
    epscol = nc.dram_tensor("epscol", [128, 1], F32, kind="ExternalInput").ap()

    y = nc.dram_tensor("y", [NQ, D], F32, kind="ExternalOutput").ap()
    dbgA = nc.dram_tensor("dbgA", [128, NH * NQ], F32, kind="ExternalOutput").ap() if dbg_attn else None
    dbgX = nc.dram_tensor("dbgX", [NQ, D], F32, kind="ExternalOutput").ap() if dbg_x2 else None
    if dbg_proj:
        dbgK = nc.dram_tensor("dbgK", [128, G * S], F32, kind="ExternalOutput").ap()
        dbgQ = nc.dram_tensor("dbgQ", [128, G * NQ], F32, kind="ExternalOutput").ap()
        dbgV = nc.dram_tensor("dbgV", [128, KC * G * HD], F32, kind="ExternalOutput").ap()
        dbgN = nc.dram_tensor("dbgN", [128, 16], F32, kind="ExternalOutput").ap()

    if dbg_att1:
        dbgD = nc.dram_tensor("dbgD", [1, 512], F32, kind="ExternalOutput").ap()
        dbgR = nc.dram_tensor("dbgR", [128, 512], F32, kind="ExternalOutput").ap()
        dbgE = nc.dram_tensor("dbgE", [128, 512], F32, kind="ExternalOutput").ap()
        dbgE0 = nc.dram_tensor("dbgE0", [128, 512], F32, kind="ExternalOutput").ap()
        dbgE1 = nc.dram_tensor("dbgE1", [128, 512], F32, kind="ExternalOutput").ap()
        dbgU = nc.dram_tensor("dbgU", [128, 512], F32, kind="ExternalOutput").ap()

    x2_scr = nc.dram_tensor("x2_scr", [NQ, D], F32, kind="Internal").ap()
    x2bf_scr = nc.dram_tensor("x2bf_scr", [NQ, D], BF16, kind="Internal").ap()

    ctx = ExitStack()

    onesbf_sb = ctx.enter_context(nc.sbuf_tensor("onesbf_sb", [1, 128], BF16, side="right"))
    onescol_sb = ctx.enter_context(nc.sbuf_tensor("onescol_sb", [128, 1], BF16, side="right"))
    onesmat_sb = ctx.enter_context(nc.sbuf_tensor("onesmat_sb", [128, 128], BF16, side="right"))
    onesf_sb = ctx.enter_context(nc.sbuf_tensor("onesf_sb", [1, 128], F32, side="right"))
    eps_sb = ctx.enter_context(nc.sbuf_tensor("eps_sb", [128, 1], F32, side="right"))

    ps = [ctx.enter_context(nc.psum_tensor(f"ps{i}", [128, 512], F32)) for i in range(4)]
    ps_sc = [ctx.enter_context(nc.psum_tensor(f"ps_sc{i}", [128, 512], F32)) for i in range(2)]
    ps_at = ctx.enter_context(nc.psum_tensor("ps_at", [128, 512], F32))
    ps_dn = ctx.enter_context(nc.psum_tensor("ps_dn", [128, 512], F32))

    s_xt = Sem(ctx, nc, "s_xt")
    s_w = Sem(ctx, nc, "s_w")
    s_pe = Sem(ctx, nc, "s_pe")
    s_sc = Sem(ctx, nc, "s_sc")
    s_ve = Sem(ctx, nc, "s_ve")
    s_dq = Sem(ctx, nc, "s_dq")
    s_sy = Sem(ctx, nc, "s_sy")
    s_out = Sem(ctx, nc, "s_out")

    sch = Sched()
    blk = ctx.enter_context(nc.Block())

    bank_free = {}

    def wfree(bid):
        return bank_free.get(bid)

    with ctx:
        st_ab = ExitStack()
        attnT = st_ab.enter_context(nc.sbuf_tensor("attnT", [128, NH * NQ], BF16))
        st_a = ExitStack()

        def sba(name, shape, dt):
            return st_a.enter_context(nc.sbuf_tensor(name, shape, dt))

        wq_b = [sba(f"wqb{i}", [128, CC * G * HD], BF16) for i in range(2)]
        wk_b = [sba(f"wkb{i}", [128, CC * G * HD], BF16) for i in range(2)]
        wv_b = [sba(f"wvb{i}", [128, CC * G * HD], BF16) for i in range(2)]
        xt_b = [sba(f"xtb{i}", [128, CC * STRIP], BF16) for i in range(2)]
        KT = sba("KT", [128, G * S], BF16)
        QTm = sba("QTm", [128, G * NQ], BF16)
        Vm = sba("Vm", [128, KC * G * HD], BF16)
        nsc = sba("nsc", [128, 2 * 512], BF16)
        runmx = sba("runmx", [1, 8], F32)
        sc11 = sba("sc11", [1, 8], F32)
        negc = sba("negc", [1, G], F32)
        negc_col = sba("negc_col", [128, G], F32)
        expS = [sba(f"expS{i}", [128, 512], BF16) for i in range(2)]
        dnln = [sba(f"dnln{i}", [128, 512], F32) for i in range(2)]
        recip_rep = [sba(f"recip_rep{i}", [128, 512], F32) for i in range(2)]

        for t_sb, t_dr in ((onesbf_sb, ones_bf), (onescol_sb, onescol_bf),
                           (onesmat_sb, onesmat_bf), (onesf_sb, ones_f), (eps_sb, epscol)):
            sch.op("gpsimd", lambda g, a=t_sb, b=t_dr: g.dma_start(a[:, :], b),
                   incs=[(s_w, 16)])

        def w_slice_dma(dst, srcT, p, waits=()):
            src = srcT.rearrange("(c p) d -> p c d", p=128)[:, :, p * G * HD:(p + 1) * G * HD]
            d3 = dst[:, :].rearrange("p (c n) -> p c n", c=CC)
            ev = None
            for half in range(2):
                cs = slice(half * 8, (half + 1) * 8)
                ev = sch.op("gpsimd",
                            lambda g, d=d3[:, cs, :], s2=src[:, cs, :]: g.dma_start(d, s2),
                            waits=waits, incs=[(s_w, 16)])
            return ev

        w_slice_dma(wq_b[0], wqT, 0)
        w_slice_dma(wk_b[0], wkT, 0)
        w_ready = [w_slice_dma(wv_b[0], wvT, 0)]

        xT_r = xT.rearrange("(c p) t -> p c t", p=128)

        strip_ev = {}
        strip_done = {}
        pass_proj_last = {}
        kq_evac_last = [None]
        attn_ev = []
        exp_free = {}
        expS_free = {}
        dump_gate = {}
        gchunk = [0]
        hqt_ctr = [0]
        dnln_free = {}
        rr_free = {}
        d_ev_last = [None]

        for p in range(npass):
            wq, wk, wv = wq_b[p % 2], wk_b[p % 2], wv_b[p % 2]
            grp_ctr = [0]
            norm_ctr = [0]
            nsc_free = {}

            mx_init_ev = sch.op("vector", lambda v: v.memset(runmx[:, 0:4], -1e30),
                                waits=[attn_ev[-1]] if attn_ev else (), incs=[(s_ve, 1)])
            norm_last = {0: mx_init_ev, 1: mx_init_ev, 2: mx_init_ev, 3: mx_init_ev}

            def proj_group(p, s, lhs_fn, rhs_fn, n_out, dst, doff, evac_eng):
                g_i = grp_ctr[0] % 4
                grp_ctr[0] += 1
                waits = [strip_ev[(p, s)], wfree(g_i), w_ready[p]]

                def fn(t, bank=ps[g_i], lhs_fn=lhs_fn, rhs_fn=rhs_fn, n_out=n_out):
                    inst = None
                    for c in range(CC):
                        inst = t.matmul(bank[:, 0:n_out], lhs_fn(c), rhs_fn(c),
                                        start=(c == 0), stop=(c == CC - 1))
                    return inst

                mm_ev = sch.op("tensor", fn, waits=waits, incs=[(s_pe, 1)])
                strip_done[(p, s)] = s_pe.n
                sem = s_ve if evac_eng == "vector" else s_sc

                def ev_fn(e, dst=dst, doff=doff, n_out=n_out, bank=ps[g_i]):
                    if evac_eng == "vector":
                        return e.tensor_copy(dst[:, doff:doff + n_out], bank[:, 0:n_out])
                    return e.copy(dst[:, doff:doff + n_out], bank[:, 0:n_out])

                evac_ev = sch.op(evac_eng, ev_fn, waits=[mm_ev], incs=[(sem, 1)])
                bank_free[g_i] = evac_ev
                return evac_ev

            def sq_and_norm(p, src, soff, col, prior_ev):
                ping = norm_ctr[0] % 2
                norm_ctr[0] += 1
                waits = [prior_ev, nsc_free.get(ping)]
                sq_ev = sch.op("scalar",
                               lambda sc, src=src, soff=soff, ping=ping: sc.activation(
                                   nsc[:, ping * 512:(ping + 1) * 512],
                                   src[:, soff:soff + 512], AF.Square),
                               waits=waits, incs=[(s_sc, 1)])
                mm_ev = sch.op("tensor",
                               lambda t, ping=ping: t.matmul(
                                   ps_dn[0:1, :], onescol_sb[:, :],
                                   nsc[:, ping * 512:(ping + 1) * 512],
                                   start=True, stop=True),
                               waits=[sq_ev, wfree("dn")], incs=[(s_pe, 1)])
                nsc_free[ping] = mm_ev
                r_ev = sch.op("vector",
                              lambda v: v.reduce_max(sc11[:, 7:8], ps_dn[0:1, :], axis=AX.X),
                              waits=[mm_ev, norm_last[col]], incs=[(s_ve, 1)])
                bank_free["dn"] = r_ev
                m_ev = sch.op("vector",
                              lambda v, col=col: v.tensor_tensor(
                                  runmx[:, col:col + 1], runmx[:, col:col + 1],
                                  sc11[:, 7:8], ALU.max),
                              waits=[r_ev], incs=[(s_ve, 1)])
                norm_last[col] = m_ev

            for s in range(NSTRIP):
                waits = []
                if (p, s) != (0, 0) and (p, s) != (0, 1):
                    pp, ss = (p, s - 2) if s >= 2 else (p - 1, NSTRIP + s - 2)
                    waits = [(s_pe, strip_done[(pp, ss)])]
                dst = xt_b[s % 2][:, :].rearrange("p (c n) -> p c n", c=CC)
                src = xT_r[:, :, s * STRIP:(s + 1) * STRIP]
                evx = None
                for half in range(2):
                    cs = slice(half * 8, (half + 1) * 8)
                    evx = sch.op("gpsimd",
                                 lambda g, d=dst[:, cs, :], sr=src[:, cs, :]: g.dma_start(d, sr),
                                 waits=waits, incs=[(s_xt, 16)])
                strip_ev[(p, s)] = evx
                xb = xt_b[s % 2]

                for dc in range(G):
                    e1 = proj_group(
                        p, s,
                        lambda c, dc=dc, wk=wk: wk[:, c * G * HD + dc * HD: c * G * HD + (dc + 1) * HD],
                        lambda c, xb=xb: xb[:, c * STRIP:(c + 1) * STRIP],
                        STRIP, KT, dc * S + s * STRIP, "vector")
                    kq_evac_last[0] = e1
                    sq_and_norm(p, KT, dc * S + s * STRIP, 2 + dc, e1)
                if s < NQ // STRIP:
                    for dc in range(G):
                        e1 = proj_group(
                            p, s,
                            lambda c, dc=dc, wq=wq: wq[:, c * G * HD + dc * HD: c * G * HD + (dc + 1) * HD],
                            lambda c, xb=xb: xb[:, c * STRIP:(c + 1) * STRIP],
                            STRIP, QTm, dc * NQ + s * STRIP, "vector")
                        kq_evac_last[0] = e1
                        sq_and_norm(p, QTm, dc * NQ + s * STRIP, dc, e1)
                for tt in range(STRIP // 128):
                    ktile = s * (STRIP // 128) + tt
                    proj_group(
                        p, s,
                        lambda c, xb=xb, tt=tt: xb[:, c * STRIP + tt * 128: c * STRIP + (tt + 1) * 128],
                        lambda c, wv=wv: wv[:, c * G * HD:(c + 1) * G * HD],
                        G * HD, Vm, ktile * G * HD, "scalar")

            pass_proj_last[p] = s_pe.n

            if p + 1 < npass:
                nxt = (p + 1) % 2
                w0 = [(s_pe, pass_proj_last[p - 1])] if p >= 1 else ()
                w_slice_dma(wq_b[nxt], wqT, p + 1, waits=w0)
                w_slice_dma(wk_b[nxt], wkT, p + 1)
                w_ready.append(w_slice_dma(wv_b[nxt], wvT, p + 1))

            neg_evs = []
            for h in range(G):
                e1 = sch.op("scalar",
                            lambda sc, h=h: sc.activation(sc11[:, 0:1], runmx[:, h:h + 1], AF.Ln),
                            waits=[norm_last[h], norm_last[2 + h]], incs=[(s_sc, 1)])
                e2 = sch.op("scalar",
                            lambda sc, h=h: sc.activation(sc11[:, 1:2], runmx[:, 2 + h:3 + h], AF.Ln),
                            waits=[e1], incs=[(s_sc, 1)])
                e3 = sch.op("vector",
                            lambda v: v.tensor_tensor(sc11[:, 2:3], sc11[:, 0:1], sc11[:, 1:2], ALU.add),
                            waits=[e2], incs=[(s_ve, 1)])
                e4 = sch.op("scalar",
                            lambda sc: sc.activation(sc11[:, 3:4], sc11[:, 2:3], AF.Exp, scale=0.5),
                            waits=[e3], incs=[(s_sc, 1)])
                e5 = sch.op("vector",
                            lambda v, h=h: v.tensor_scalar(
                                negc[:, h:h + 1], sc11[:, 3:4], -1.01 * SCALE, None, ALU.mult),
                            waits=[e4], incs=[(s_ve, 1)])
                neg_evs.append(e5)
            rep_mm = sch.op("tensor",
                            lambda t: t.matmul(ps_at[:, 0:G], onesf_sb[:, :], negc[:, 0:G],
                                               start=True, stop=True),
                            waits=neg_evs + [wfree("at")], incs=[(s_pe, 1)])
            rep_cp = sch.op("scalar",
                            lambda sc: sc.copy(negc_col[:, :], ps_at[:, 0:G]),
                            waits=[rep_mm], incs=[(s_sc, 1)])
            bank_free["at"] = rep_cp

            for h in range(1 if dbg_att1 else G):
                hd = p * G + h
                for qt in range(1 if dbg_att1 else NQT):
                    qoff = qt * QTW
                    pend = []
                    par = hqt_ctr[0] % 2
                    hqt_ctr[0] += 1
                    at_bank, at_key = (ps_at, "at") if par == 0 else (ps[2], 2)
                    dn_bank, dn_key = (ps_dn, "dn") if par == 0 else (ps[3], 3)

                    def emit_attn_chunk(kc, ping, e_ev):
                        aw = [e_ev]
                        if kc == 0:
                            aw.append(wfree(at_key))

                        def mm_a(t, kc=kc, h=h, ping=ping, at_bank=at_bank):
                            return t.matmul(at_bank[:, :],
                                            Vm[:, kc * G * HD + h * HD: kc * G * HD + (h + 1) * HD],
                                            expS[ping][:, :],
                                            start=(kc == 0), stop=(kc == KC - 1))

                        a_ev = sch.op("tensor", mm_a, waits=aw, incs=[(s_pe, 1)])
                        dw = [wfree(dn_key)] if kc == 0 else []

                        def mm_d(t, kc=kc, ping=ping, dn_bank=dn_bank):
                            return t.matmul(dn_bank[:, :], onesmat_sb[:, :], expS[ping][:, :],
                                            start=(kc == 0), stop=(kc == KC - 1))

                        d_ev = sch.op("tensor", mm_d, waits=dw, incs=[(s_pe, 1)])
                        expS_free[ping] = d_ev
                        return a_ev, d_ev

                    a_ev = d_ev = None
                    for kc in range(KC):
                        ping = gchunk[0] % 2
                        gchunk[0] += 1
                        sw = [exp_free.get(ping)]
                        if kc == 0 and qt == 0 and h == 0:
                            sw.append(kq_evac_last[0])

                        def mm_s(t, h=h, kc=kc, qoff=qoff, ping=ping):
                            return t.matmul(ps_sc[ping][:, :],
                                            KT[:, h * S + kc * 128:h * S + (kc + 1) * 128],
                                            QTm[:, h * NQ + qoff:h * NQ + qoff + QTW],
                                            start=True, stop=True)

                        s_ev = sch.op("tensor", mm_s, waits=sw, incs=[(s_pe, 1)])

                        ew_ = [s_ev, expS_free.get(ping), dump_gate.pop(ping, None)]
                        if kc == 0 and qt == 0 and h == 0:
                            ew_.append(rep_cp)
                        e_ev = sch.op("scalar",
                                      lambda sc, ping=ping, h=h: sc.activation(
                                          expS[ping][:, :], ps_sc[ping][:, :], AF.Exp,
                                          bias=negc_col[:, h:h + 1], scale=SCALE),
                                      waits=ew_, incs=[(s_sc, 1)])
                        exp_free[ping] = e_ev
                        if dbg_att1 and p == 0 and h == 0 and qt == 0 and kc in (0, 16):
                            dst = dbgE0 if kc == 0 else dbgE1
                            dev_ = sch.op("gpsimd",
                                          lambda g, dst=dst, ping=ping: g.dma_start(
                                              dst, expS[ping][:, :]),
                                          waits=[e_ev], incs=[(s_out, 16)])
                            dump_gate[ping] = dev_
                        pend.append((kc, ping, e_ev))
                        if len(pend) >= 2:
                            kc0, ping0, e0 = pend.pop(0)
                            a_ev, d_ev = emit_attn_chunk(kc0, ping0, e0)
                    while pend:
                        kc0, ping0, e0 = pend.pop(0)
                        a_ev, d_ev = emit_attn_chunk(kc0, ping0, e0)

                    r1 = sch.op("scalar",
                                lambda sc, dn_bank=dn_bank, par=par: sc.copy(
                                    dnln[par][:, :], dn_bank[:, :]),
                                waits=[d_ev, dnln_free.get(par)], incs=[(s_sc, 1)])
                    bank_free[dn_key] = r1
                    r2w = [r1, rr_free.get(par)]
                    r2 = sch.op("vector",
                                lambda v, par=par: v.reciprocal(recip_rep[par][:, :],
                                                                dnln[par][:, :]),
                                waits=r2w, incs=[(s_ve, 1)])
                    dnln_free[par] = r2
                    t_ev = sch.op("vector",
                                  lambda v, hd=hd, qoff=qoff, at_bank=at_bank, par=par:
                                  v.tensor_tensor(
                                      attnT[:, hd * NQ + qoff: hd * NQ + qoff + QTW],
                                      at_bank[:, :], recip_rep[par][:, :], ALU.mult),
                                  waits=[a_ev, r2], incs=[(s_ve, 1)])
                    attn_ev.append(t_ev)
                    bank_free[at_key] = t_ev
                    rr_free[par] = t_ev
                    if dbg_att1 and p == 0 and h == 0 and qt == 0:
                        sch.op("gpsimd", lambda g, par=par: g.dma_start(dbgR, recip_rep[par][:, :]),
                               waits=[t_ev], incs=[(s_out, 16)])
                        sch.op("gpsimd",
                               lambda g: g.dma_start(dbgE, expS[(gchunk[0] - 1) % 2][:, :]),
                               waits=[t_ev], incs=[(s_out, 16)])

        phA = [(s_pe, s_pe.n), (s_ve, s_ve.n), (s_sc, s_sc.n)]

        if dbg_proj:
            wv_ = [(s_pe, s_pe.n), (s_ve, s_ve.n), (s_sc, s_sc.n)]
            sch.op("gpsimd", lambda g: g.dma_start(dbgK, KT[:, :]), waits=wv_, incs=[(s_out, 16)])
            sch.op("gpsimd", lambda g: g.dma_start(dbgQ, QTm[:, :]), incs=[(s_out, 16)])
            sch.op("gpsimd", lambda g: g.dma_start(dbgV, Vm[:, :]), incs=[(s_out, 16)])
            sch.op("gpsimd", lambda g: g.dma_start(dbgN[:, 0:G], negc_col[:, :]), incs=[(s_out, 16)])
            sch.op("gpsimd", lambda g: g.dma_start(dbgN[0:1, 8:16], runmx[:, :]), incs=[(s_out, 16)])

        if dbg_attn:
            dmp = sch.op("gpsimd", lambda g: g.dma_start(dbgA, attnT[:, :]),
                         waits=[attn_ev[-1]], incs=[(s_out, 16)])
            sch.op("sync", lambda s2: s2.nop(), waits=[(s_out, s_out.n)])

        if not do_bc:
            sch.emit(blk)
            st_a.close()
            st_ab.close()
            return nc

        st_a.close()
        X2T = ctx.enter_context(nc.sbuf_tensor("X2T", [128, CC * 1024], BF16, side="right"))
        x2bf = [ctx.enter_context(nc.sbuf_tensor(f"x2bf{i}", [128, D], BF16, side="right"))
                for i in range(2)]
        st_b = ExitStack()
        woT_b = st_b.enter_context(nc.sbuf_tensor("woT_b", [128, CC * D], BF16))
        xres = [st_b.enter_context(nc.sbuf_tensor(f"xres{i}", [128, D], F32)) for i in range(2)]
        cen = [st_b.enter_context(nc.sbuf_tensor(f"cen{i}", [128, D], F32)) for i in range(2)]
        nmean = [st_b.enter_context(nc.sbuf_tensor(f"nmean{i}", [128, 1], F32)) for i in range(2)]
        sumsq = [st_b.enter_context(nc.sbuf_tensor(f"sumsq{i}", [128, 1], F32)) for i in range(2)]
        rstd = [st_b.enter_context(nc.sbuf_tensor(f"rstd{i}", [128, 1], F32)) for i in range(2)]
        sqd = st_b.enter_context(nc.sbuf_tensor("sqd", [128, D], BF16))

        woT_ev = []
        for hdc in range(CC):
            wv_ = phA if hdc == 0 else ()
            woT_ev.append(sch.op("gpsimd",
                                 lambda g, hdc=hdc: g.dma_start(
                                     woT_b[:, hdc * D:(hdc + 1) * D],
                                     woT[hdc * 128:(hdc + 1) * 128, :]),
                                 waits=wv_, incs=[(s_w, 16)]))

        resid_ev = {}
        cen_free = {}
        x2w_ev = {}
        x2bf_free = {}
        tposed_ev = {}
        ln_free = {}

        for tt in range(NTT):
            pi = tt % 2
            xr_ev = sch.op("scalar",
                           lambda sc, tt=tt, pi=pi: sc.dma_start(
                               xres[pi][:, :], xq[tt * 128:(tt + 1) * 128, :]),
                           waits=[resid_ev.get(tt - 2)], incs=[(s_dq, 16)])
            g_evs = []
            for dmt in range(4):
                waits = [wfree(dmt)]
                if tt == 0:
                    waits += phA

                def fn(t, dmt=dmt, tt=tt):
                    inst = None
                    for hdc in range(CC):
                        if tt == 0 and dmt == 0:
                            t.wait_ge(woT_ev[hdc][0].h, woT_ev[hdc][1])
                        inst = t.matmul(ps[dmt][:, :],
                                        attnT[:, hdc * NQ + tt * 128:hdc * NQ + (tt + 1) * 128],
                                        woT_b[:, hdc * D + dmt * 512:hdc * D + (dmt + 1) * 512],
                                        start=(hdc == 0), stop=(hdc == CC - 1))
                    return inst

                g_evs.append(sch.op("tensor", fn, waits=waits, incs=[(s_pe, 1)]))
            r_evs = []
            for dmt in range(4):
                r_evs.append(sch.op("vector",
                                    lambda v, dmt=dmt, pi=pi: v.tensor_tensor(
                                        cen[pi][:, dmt * 512:(dmt + 1) * 512],
                                        ps[dmt][:, :],
                                        xres[pi][:, dmt * 512:(dmt + 1) * 512], ALU.add),
                                    waits=[g_evs[dmt], xr_ev, cen_free.get(tt - 2)],
                                    incs=[(s_ve, 1)]))
                bank_free[dmt] = r_evs[-1]
            resid_ev[tt] = r_evs[-1]

            l1 = sch.op("vector",
                        lambda v, pi=pi: v.reduce_sum(nmean[pi][:, :], cen[pi][:, :], axis=AX.X, negate=True),
                        waits=r_evs + [ln_free.get(pi)], incs=[(s_ve, 1)])
            l2 = sch.op("vector",
                        lambda v, pi=pi: v.tensor_scalar(nmean[pi][:, :], nmean[pi][:, :], 1.0 / D, None, ALU.mult),
                        waits=[l1], incs=[(s_ve, 1)])
            l3 = sch.op("vector",
                        lambda v, pi=pi: v.tensor_scalar(cen[pi][:, :], cen[pi][:, :],
                                                         nmean[pi][:, :], None, ALU.add),
                        waits=[l2], incs=[(s_ve, 1)])
            l4 = sch.op("scalar",
                        lambda sc, pi=pi: sc.activation(sqd[:, :], cen[pi][:, :], AF.Square,
                                                        accum_out=sumsq[pi][:, :]),
                        waits=[l3], incs=[(s_sc, 1)])
            l5 = sch.op("scalar",
                        lambda sc, pi=pi: sc.activation(sumsq[pi][:, :], sumsq[pi][:, :], AF.Ln,
                                                 bias=eps_sb[:, 0:1], scale=1.0 / D),
                        waits=[l4], incs=[(s_sc, 1)])
            l6 = sch.op("scalar",
                        lambda sc, pi=pi: sc.activation(rstd[pi][:, :], sumsq[pi][:, :], AF.Exp, scale=-0.5),
                        waits=[l5], incs=[(s_sc, 1)])
            l7 = sch.op("vector",
                        lambda v, pi=pi: v.tensor_scalar(cen[pi][:, :], cen[pi][:, :],
                                                         rstd[pi][:, :], None, ALU.mult),
                        waits=[l6], incs=[(s_ve, 1)])
            l8 = sch.op("vector",
                        lambda v, pi=pi: v.tensor_copy(x2bf[pi][:, :], cen[pi][:, :]),
                        waits=[l7, x2bf_free.get(tt - 2)], incs=[(s_ve, 1)])
            ln_free[pi] = l7

            w1 = sch.op("scalar",
                        lambda sc, tt=tt, pi=pi: sc.dma_start(
                            x2_scr[tt * 128:(tt + 1) * 128, :], cen[pi][:, :]),
                        waits=[l7], incs=[(s_dq, 16)])
            x2w_ev[tt] = w1
            cen_free[tt] = w1
            if tt >= 8:
                w2 = sch.op("scalar",
                            lambda sc, tt=tt, pi=pi: sc.dma_start(
                                x2bf_scr[tt * 128:(tt + 1) * 128, :], x2bf[pi][:, :]),
                            waits=[l8], incs=[(s_dq, 16)])
                x2bf_free[tt] = w2
            else:
                tp = None
                for c in range(CC):
                    tp = sch.op("sync",
                                lambda s2, c=c, tt=tt, pi=pi: s2.dma_start(
                                    X2T[:, c * 1024 + tt * 128:c * 1024 + (tt + 1) * 128],
                                    x2bf[pi][:, c * 128:(c + 1) * 128], transpose=True),
                                waits=[l8] if c == 0 else (), incs=[(s_sy, 16)])
                tposed_ev[tt] = tp
                x2bf_free[tt] = tp

            if dbg_x2:
                sch.op("gpsimd",
                       lambda g, tt=tt, pi=pi: g.dma_start(
                           dbgX[tt * 128:(tt + 1) * 128, :], cen[pi][:, :]),
                       waits=[l7], incs=[(s_out, 16)])

        phB = [(s_pe, s_pe.n), (s_ve, s_ve.n), (s_sc, s_sc.n), (s_dq, s_dq.n), (s_sy, s_sy.n)]

        st_b.close()
        st_ab.close()
        st_c = ExitStack()

        def sbc(name, shape, dt):
            return st_c.enter_context(nc.sbuf_tensor(name, shape, dt))

        acc = sbc("acc", [128, 8 * D], F32)
        We_b = [sbc(f"Web{i}", [128, CC * 512], BF16) for i in range(2)]
        tmp = [sbc(f"tmp{i}", [128, 512], F32) for i in range(2)]
        yb = [sbc(f"yb{i}", [128, D], F32) for i in range(2)]
        gwb = sbc("gwb", [128, CC * NE], BF16)
        gbb = sbc("gbb", [1, NE], BF16)
        ebb = [sbc(f"ebb{i}", [1, D], BF16) for i in range(2)]
        zg = sbc("zg", [128, NE], F32)
        zz = sbc("zz", [128, NE], F32)
        m1 = sbc("m1", [128, 1], F32)
        m2 = sbc("m2", [128, 1], F32)
        mask = sbc("mask", [128, 8 * NE], F32)
        nmean2 = sbc("nmean2", [128, 1], F32)
        sumsq2 = sbc("sumsq2", [128, 1], F32)
        rstd2 = sbc("rstd2", [128, 1], F32)
        sqd2 = sbc("sqd2", [128, D], BF16)

        gw_ev = sch.op("gpsimd", lambda g: g.dma_start(gwb[:, :], gwt), waits=phB, incs=[(s_w, 16)])
        gb_ev = sch.op("gpsimd", lambda g: g.dma_start(gbb[:, :], gb), incs=[(s_w, 16)])

        grp_c = [0]
        acc_w = {}
        tmp_free = {}
        We_free = {}
        ebb_free = {}
        yb_free = {}
        acc_init_ev = {}
        ln2_chain = [None]
        halfC_pe_done = [None]
        tpose_c = {}
        x2bf_ld = {}
        last_y = []

        for H in range(2):
            if H == 1:
                for ti in range(8):
                    tt = 8 + ti
                    pi = tt % 2
                    ld = sch.op("scalar",
                                lambda sc, tt=tt, pi=pi: sc.dma_start(
                                    x2bf[pi][:, :], x2bf_scr[tt * 128:(tt + 1) * 128, :]),
                                waits=[x2bf_free.get(tt - 2), x2w_ev.get(tt),
                                       (s_dq, phB[3][1])], incs=[(s_dq, 16)])
                    tp = None
                    for c in range(CC):
                        wv_ = [ld]
                        if ti == 0 and c == 0:
                            wv_.append((s_pe, halfC_pe_done[0]))
                        tp = sch.op("sync",
                                    lambda s2, c=c, ti=ti, pi=pi: s2.dma_start(
                                        X2T[:, c * 1024 + ti * 128:c * 1024 + (ti + 1) * 128],
                                        x2bf[pi][:, c * 128:(c + 1) * 128], transpose=True),
                                    waits=wv_, incs=[(s_sy, 16)])
                    tpose_c[tt] = tp
                    x2bf_free[tt] = tp

            for ti in range(8):
                tt = H * 8 + ti
                wv_ = [x2w_ev[tt], acc_w.get(("ln2", ti))]
                if H == 0 and ti == 0:
                    wv_ += phB
                acc_init_ev[ti] = sch.op("scalar",
                                         lambda sc, tt=tt, ti=ti: sc.dma_start(
                                             acc[:, ti * D:(ti + 1) * D],
                                             x2_scr[tt * 128:(tt + 1) * 128, :]),
                                         waits=wv_, incs=[(s_dq, 16)])

            mask_ev = {}
            for ti in range(8):
                tt = H * 8 + ti
                g_i = grp_c[0] % 4
                grp_c[0] += 1
                wv_ = [gw_ev, gb_ev, wfree(g_i),
                       tposed_ev.get(tt) if H == 0 else tpose_c.get(tt)]

                def fng(t, ti=ti, g_i=g_i):
                    t.matmul(ps[g_i][:, 0:NE], onesbf_sb[:, :], gbb[:, :], start=True, stop=False)
                    inst = None
                    for c in range(CC):
                        inst = t.matmul(ps[g_i][:, 0:NE],
                                        X2T[:, c * 1024 + ti * 128:c * 1024 + (ti + 1) * 128],
                                        gwb[:, c * NE:(c + 1) * NE],
                                        start=False, stop=(c == CC - 1))
                    return inst

                gm = sch.op("tensor", fng, waits=wv_, incs=[(s_pe, 1)])
                zev = sch.op("scalar", lambda sc, g_i=g_i: sc.copy(zg[:, :], ps[g_i][:, 0:NE]),
                             waits=[gm, mask_ev.get(ti - 1)], incs=[(s_sc, 1)])
                bank_free[g_i] = zev
                c1 = sch.op("vector", lambda v: v.reduce_max(m1[:, :], zg[:, :], axis=AX.X),
                            waits=[zev], incs=[(s_ve, 1)])
                c2 = sch.op("vector",
                            lambda v: v.tensor_scalar(zz[:, :], zg[:, :], m1[:, :], None, ALU.is_equal),
                            waits=[c1], incs=[(s_ve, 1)])
                c3 = sch.op("vector",
                            lambda v: v.scalar_tensor_tensor(zz[:, :], zz[:, :], -1e30, zg[:, :],
                                                             ALU.mult, ALU.add),
                            waits=[c2], incs=[(s_ve, 1)])
                c4 = sch.op("vector", lambda v: v.reduce_max(m2[:, :], zz[:, :], axis=AX.X),
                            waits=[c3], incs=[(s_ve, 1)])
                c5 = sch.op("vector",
                            lambda v, ti=ti: v.tensor_scalar(mask[:, ti * NE:(ti + 1) * NE],
                                                             zg[:, :], m2[:, :], 0.5,
                                                             ALU.is_ge, ALU.mult),
                            waits=[c4], incs=[(s_ve, 1)])
                mask_ev[ti] = c5

            for e in range(NE):
                wpi = e % 2
                eb_ev = sch.op("gpsimd",
                               lambda g, e=e, wpi=wpi: g.dma_start(
                                   ebb[wpi][:, :], eb[0:1, e * D:(e + 1) * D]),
                               waits=[ebb_free.get(wpi)], incs=[(s_w, 16)])
                for oq in range(4):
                    bpi = (e * 4 + oq) % 2
                    src = ew[e].rearrange("(c p) o -> p c o", p=128)[:, :, oq * 512:(oq + 1) * 512]
                    d3 = We_b[bpi][:, :].rearrange("p (c n) -> p c n", c=CC)
                    we_ev = None
                    for half in range(2):
                        cs = slice(half * 8, (half + 1) * 8)
                        we_ev = sch.op("gpsimd",
                                       lambda g, d=d3[:, cs, :], s2=src[:, cs, :]: g.dma_start(d, s2),
                                       waits=[We_free.get(bpi)],
                                       incs=[(s_w, 16)])
                    for ti in range(8):
                        g_i = grp_c[0] % 4
                        grp_c[0] += 1
                        wv_ = [we_ev, eb_ev, wfree(g_i)]

                        def fne(t, ti=ti, g_i=g_i, e=e, oq=oq, wpi=wpi, bpi=bpi):
                            t.matmul(ps[g_i][:, :], onesbf_sb[:, :],
                                     ebb[wpi][:, oq * 512:(oq + 1) * 512], start=True, stop=False)
                            inst = None
                            for c in range(CC):
                                inst = t.matmul(ps[g_i][:, :],
                                                X2T[:, c * 1024 + ti * 128:c * 1024 + (ti + 1) * 128],
                                                We_b[bpi][:, c * 512:(c + 1) * 512],
                                                start=False, stop=(c == CC - 1))
                            return inst

                        hm = sch.op("tensor", fne, waits=wv_, incs=[(s_pe, 1)])
                        if ti == 7:
                            We_free[bpi] = hm
                            if oq == 3:
                                ebb_free[wpi] = hm
                        tpi = grp_c[0] % 2
                        rl = sch.op("scalar",
                                    lambda sc, g_i=g_i, ti=ti, e=e, tpi=tpi: sc.activation(
                                        tmp[tpi][:, :], ps[g_i][:, :], AF.Relu,
                                        scale=mask[:, ti * NE + e:ti * NE + e + 1]),
                                    waits=[hm, mask_ev[ti], tmp_free.get(tpi)],
                                    incs=[(s_sc, 1)])
                        bank_free[g_i] = rl
                        ac = sch.op("vector",
                                    lambda v, ti=ti, oq=oq, tpi=tpi: v.tensor_tensor(
                                        acc[:, ti * D + oq * 512:ti * D + (oq + 1) * 512],
                                        acc[:, ti * D + oq * 512:ti * D + (oq + 1) * 512],
                                        tmp[tpi][:, :], ALU.add),
                                    waits=[rl, acc_w.get((ti, oq), acc_init_ev[ti])],
                                    incs=[(s_ve, 1)])
                        acc_w[(ti, oq)] = ac
                        tmp_free[tpi] = ac
            halfC_pe_done[0] = s_pe.n

            for ti in range(8):
                tt = H * 8 + ti
                pi = ti % 2
                aw = [acc_w[(ti, oq)] for oq in range(4)]
                l1 = sch.op("vector",
                            lambda v, ti=ti: v.reduce_sum(nmean2[:, :], acc[:, ti * D:(ti + 1) * D],
                                                          axis=AX.X, negate=True),
                            waits=aw + [ln2_chain[0]], incs=[(s_ve, 1)])
                l2 = sch.op("vector",
                            lambda v: v.tensor_scalar(nmean2[:, :], nmean2[:, :], 1.0 / D, None, ALU.mult),
                            waits=[l1], incs=[(s_ve, 1)])
                l3 = sch.op("vector",
                            lambda v, ti=ti: v.tensor_scalar(acc[:, ti * D:(ti + 1) * D],
                                                             acc[:, ti * D:(ti + 1) * D],
                                                             nmean2[:, :], None, ALU.add),
                            waits=[l2], incs=[(s_ve, 1)])
                l4 = sch.op("scalar",
                            lambda sc, ti=ti: sc.activation(sqd2[:, :], acc[:, ti * D:(ti + 1) * D],
                                                            AF.Square, accum_out=sumsq2[:, :]),
                            waits=[l3], incs=[(s_sc, 1)])
                l5 = sch.op("scalar",
                            lambda sc: sc.activation(sumsq2[:, :], sumsq2[:, :], AF.Ln,
                                                     bias=eps_sb[:, 0:1], scale=1.0 / D),
                            waits=[l4], incs=[(s_sc, 1)])
                l6 = sch.op("scalar",
                            lambda sc: sc.activation(rstd2[:, :], sumsq2[:, :], AF.Exp, scale=-0.5),
                            waits=[l5], incs=[(s_sc, 1)])
                l7 = sch.op("vector",
                            lambda v, ti=ti, pi=pi: v.tensor_scalar(
                                yb[pi][:, :], acc[:, ti * D:(ti + 1) * D],
                                rstd2[:, :], None, ALU.mult),
                            waits=[l6, yb_free.get(pi)], incs=[(s_ve, 1)])
                ln2_chain[0] = l7
                acc_w[("ln2", ti)] = l7
                yw = sch.op("scalar",
                            lambda sc, tt=tt, pi=pi: sc.dma_start(
                                y[tt * 128:(tt + 1) * 128, :], yb[pi][:, :]),
                            waits=[l7], incs=[(s_out, 16)])
                yb_free[pi] = yw
                last_y.append(yw)

        sch.op("sync", lambda s2: s2.nop(), waits=[(s_out, s_out.n)])

        sch.emit(blk)
        st_c.close()
    return nc


_NC_CACHE = {}


def _get_nc():
    if "nc" not in _NC_CACHE:
        _NC_CACHE["nc"] = build_program()
    return _NC_CACHE["nc"]


def kernel(L0_attn_hidden_states, L0_attn_Wq, L0_attn_Wk, L0_attn_Wv, L0_attn_Wo,
           L0_ffn_gate_w, L0_ffn_gate_b, L0_ffn_expert_w, L0_ffn_expert_b):
    x = np.ascontiguousarray(np.asarray(L0_attn_hidden_states, dtype=np.float32))
    Wq = np.asarray(L0_attn_Wq, dtype=np.float32)
    Wk = np.asarray(L0_attn_Wk, dtype=np.float32)
    Wv = np.asarray(L0_attn_Wv, dtype=np.float32)
    Wo = np.asarray(L0_attn_Wo, dtype=np.float32)
    gate_w = np.asarray(L0_ffn_gate_w, dtype=np.float32)
    gate_b = np.asarray(L0_ffn_gate_b, dtype=np.float32)
    expert_w = np.ascontiguousarray(np.asarray(L0_ffn_expert_w, dtype=np.float32))
    expert_b = np.asarray(L0_ffn_expert_b, dtype=np.float32)

    wqT = np.ascontiguousarray(Wq.T)
    wkT = np.ascontiguousarray(Wk.T)
    wvT = np.ascontiguousarray(Wv.T)
    woT = np.ascontiguousarray(Wo.T)
    gwt = np.ascontiguousarray(gate_w.reshape(CC, 128, NE).transpose(1, 0, 2).reshape(128, CC * NE))
    gb1 = np.ascontiguousarray(gate_b.reshape(1, NE))
    eb1 = np.ascontiguousarray(expert_b.reshape(1, NE * D))
    ones_bf = np.ones((1, 128), dtype=ml_dtypes.bfloat16)
    onescol_bf = np.ones((128, 1), dtype=ml_dtypes.bfloat16)
    onesmat_bf1 = np.ones((128, 128), dtype=ml_dtypes.bfloat16)
    ones_f = np.ones((1, 128), dtype=np.float32)
    epscol = np.full((128, 1), EPS, dtype=np.float32)

    nc = _get_nc()
    in_maps = []
    for core in range(8):
        b, hf = core // 2, core % 2
        own = x[b, hf * NQ:(hf + 1) * NQ]
        oth = x[b, (1 - hf) * NQ:(2 - hf) * NQ]
        xroll = np.concatenate([own, oth], axis=0)
        in_maps.append({
            "xT": np.ascontiguousarray(xroll.T),
            "xq": np.ascontiguousarray(own),
            "wqT": wqT, "wkT": wkT, "wvT": wvT, "woT": woT,
            "gwt": gwt, "gb": gb1, "ew": expert_w, "eb": eb1,
            "ones_bf": ones_bf, "onescol_bf": onescol_bf, "onesmat_bf": onesmat_bf1,
            "ones_f": ones_f, "epscol": epscol,
        })

    res = run_bass_kernel_spmd(nc, in_maps, core_ids=list(range(8)))
    out = np.empty((4, S, D), dtype=np.float32)
    for core in range(8):
        b, hf = core // 2, core % 2
        out[b, hf * NQ:(hf + 1) * NQ] = res.results[core]["y"]
    return out


if __name__ == "__main__":
    build_program()
    print("build ok")
